# version 04
# baseline (speedup 1.0000x reference)
import os
import numpy as np
import ml_dtypes
import concourse.bass as bass
import concourse.mybir as mybir
from concourse.bass_utils import run_bass_kernel_spmd
from contextlib import ExitStack

F32 = mybir.dt.float32
BF16 = mybir.dt.bfloat16
AF = mybir.ActivationFunctionType
ALU = mybir.AluOpType
AX = mybir.AxisListType
bfd = ml_dtypes.bfloat16

B, T, C = 4, 2048, 2048
H, KVH, HD = 16, 4, 128
NT, NL, TL = 16, 8, 1024
HF = 8192
SC = 1.0 / np.sqrt(128.0)
MAGIC = 1.5 * 2.0 ** 23
EPS_LN = 1e-5
EPS_RMS = 1e-8
NEG = -1.0e30
TILES = {0: [0, 3, 4, 7, 8, 11, 12, 15], 1: [1, 2, 5, 6, 9, 10, 13, 14]}
NKP = [2 * (l + 1) for l in range(NL)]


def _hilo(a):
    hi = a.astype(bfd).astype(np.float32)
    lo = (a - hi).astype(bfd).astype(np.float32)
    return hi, lo


def _slopes():
    start = 2.0 ** (-(2.0 ** -(np.log2(H) - 3.0)))
    return (start ** (np.arange(H) + 1)).astype(np.float64)


def _tile_w(w, fsz):
    K, F = w.shape
    a = w.reshape(K // 128, 128, F // fsz, fsz)
    return np.ascontiguousarray(a.transpose(2, 1, 0, 3)).reshape(
        F // fsz * 128, K // 128 * fsz)


def stage_core(inp, core):
    b, par = core // 2, core % 2
    tiles = TILES[par]
    x = np.asarray(inp["L0_attn_x"][b], dtype=np.float32)
    own = np.concatenate([np.arange(g * 128, (g + 1) * 128) for g in tiles])
    sl = _slopes()
    s_hi, s_lo = _hilo(sl.astype(np.float32))
    slopes4 = np.zeros((4, H * 128), dtype=np.float32)
    for h in range(H):
        slopes4[0, h * 128:(h + 1) * 128] = s_hi[h]
        slopes4[1, h * 128:(h + 1) * 128] = s_hi[h]
        slopes4[2, h * 128:(h + 1) * 128] = s_lo[h]
        slopes4[3, h * 128:(h + 1) * 128] = s_lo[h]
    biascol = (np.asarray(sl)[None, :] * np.arange(128)[:, None]).astype(np.float32)
    augrows = np.zeros((4, NL * 16 * 128), dtype=np.float32)
    for l in range(NL):
        G = tiles[l]
        for j in range(NKP[l]):
            off = (l * 16 + j) * 128
            if j <= G:
                r = (128.0 * (j - G) - np.arange(128)).astype(np.float32)
                hi, lo = _hilo(r)
            else:
                hi = np.full(128, NEG, dtype=np.float32)
                lo = np.full(128, NEG, dtype=np.float32)
            augrows[0, off:off + 128] = hi
            augrows[1, off:off + 128] = lo
            augrows[2, off:off + 128] = hi
            augrows[3, off:off + 128] = lo
    masktiles = np.zeros((128, NL * 256), dtype=np.float32)
    tri = np.where(np.arange(128)[:, None] > np.arange(128)[None, :], NEG, 0.0
                   ).astype(np.float32)
    for l in range(NL):
        G, nk = tiles[l], NKP[l]
        for slot, j in enumerate((nk - 2, nk - 1)):
            if j == G:
                masktiles[:, l * 256 + slot * 128: l * 256 + (slot + 1) * 128] = tri
    bo_hi, bo_lo = _hilo(np.asarray(inp["L0_attn_bo"], dtype=np.float32))
    return {
        "xT": np.ascontiguousarray(x.T),
        "xTq": np.ascontiguousarray(x.T[:, own]),
        "xres": np.ascontiguousarray(x[own]),
        "wq": _tile_w(np.asarray(inp["L0_attn_Wq"], np.float32), 128),
        "wkvk": _tile_w(np.asarray(inp["L0_attn_Wkv"], np.float32)[:, :512], 512),
        "wkvv": _tile_w(np.asarray(inp["L0_attn_Wkv"], np.float32)[:, 512:], 512),
        "wo": _tile_w(np.asarray(inp["L0_attn_Wo"], np.float32), 512),
        "bo2": np.stack([bo_hi, bo_lo]),
        "slopes4": slopes4, "biascol": biascol,
        "augrows": augrows, "masktiles": masktiles,
        "wgt": _tile_w(np.asarray(inp["L0_ffn_W_g"], np.float32).T, 128),
        "wut": _tile_w(np.asarray(inp["L0_ffn_W_u"], np.float32).T, 128),
        "wdt": _tile_w(np.asarray(inp["L0_ffn_W_d"], np.float32).T, 128),
    }


def build(stage=5):
    nc = bass.Bass("TRN2", target_bir_lowering=False)
    D = {}
    for nm, shape, dt in (
            ("xT", [C, T], F32), ("xTq", [C, TL], F32), ("xres", [TL, C], F32),
            ("wq", [C, C], F32), ("wkvk", [128, 16 * 512], F32),
            ("wkvv", [128, 16 * 512], F32), ("wo", [512, 16 * 512], F32),
            ("bo2", [2, C], F32), ("slopes4", [4, H * 128], F32),
            ("biascol", [128, 16], F32), ("augrows", [4, NL * 16 * 128], F32),
            ("masktiles", [128, NL * 256], F32),
            ("wgt", [HF, C], F32), ("wut", [HF, C], F32),
            ("wdt", [C, HF], F32)):
        D[nm] = nc.dram_tensor(nm, shape, dt, kind="ExternalInput").ap()
    D["y"] = nc.dram_tensor("y", [TL, C], F32, kind="ExternalOutput").ap()
    sk3 = "ExternalOutput" if stage in (3, 4) else "Internal"
    sk4 = "ExternalOutput" if stage in (4, 45) else "Internal"
    D["x1d"] = nc.dram_tensor("x1d", [TL, C], F32, kind=sk3).ap()
    D["xqd"] = nc.dram_tensor("xqd", [TL, C], BF16, kind=sk3).ap()
    D["y2d"] = nc.dram_tensor("y2d", [TL, C], F32, kind=sk4).ap()
    dbg = {}
    if stage == 1:
        dbg["q"] = nc.dram_tensor("dbg_q", [H * 128, TL], BF16, kind="ExternalOutput").ap()
        dbg["k"] = nc.dram_tensor("dbg_k", [KVH * 128, T], BF16, kind="ExternalOutput").ap()
        dbg["v"] = nc.dram_tensor("dbg_v", [T, 512], BF16, kind="ExternalOutput").ap()
    if stage == 2:
        dbg["attn"] = nc.dram_tensor("dbg_attn", [H * 128, TL], BF16, kind="ExternalOutput").ap()
    if stage == 34:
        dbg["p"] = nc.dram_tensor("dbg_p", [64 * 128, TL], BF16, kind="ExternalOutput").ap()

    class Ct:
        def __init__(self, name):
            self.sem = nc.semaphore(name).__enter__()
            self.n = 0
        def inc(self, ins, k=1):
            ins.then_inc(self.sem, k)
            self.n += k
            return self.n
        def expect(self, v):
            assert self.n == v, (self.n, v)

    SG, SS, SP, SA, SV = Ct("SG"), Ct("SS"), Ct("SP"), Ct("SA"), Ct("SV")

    def gdma(g, dst, src):
        return SG.inc(g.dma_start(dst, src), 16)

    def sdma(s, dst, src, **kw):
        return SS.inc(s.dma_start(dst, src, **kw), 16)

    def gdma3(g, dst, src, inner):
        return SG.inc(g.dma_start(
            dst[:].rearrange("p (c f) -> p c f", f=inner),
            src.rearrange("(c p) f -> p c f", p=128)), 16)

    with ExitStack() as SCO:
        def sb(st, name, shape, dt, side=None):
            return st.enter_context(nc.sbuf_tensor(name, shape, dt, side=side))
        def psm(st, name, shape):
            return st.enter_context(nc.psum_tensor(name, shape, F32))
        ones2 = sb(SCO, "ones2", [2, 128], BF16)
        onesf = sb(SCO, "onesf", [128, 128], BF16)
        bo2_s = sb(SCO, "bo2_s", [2, C], BF16)

        with ExitStack() as SCA:
            oa_fm = [sb(SCA, f"oafm{h}", [128, TL], BF16) for h in range(H)]
            SCB = SCA.enter_context(ExitStack())
            q_fm = [sb(SCB, f"qfm{h}", [128, TL], BF16) for h in range(H)]
            k_fm = [sb(SCB, f"kfm{g}", [128, T], BF16) for g in range(KVH)]
            v_tok = [sb(SCB, f"vtok{j}", [128, 512], BF16) for j in range(NT)]

            with ExitStack() as S1:
                x_fm = [sb(S1, f"xfm{d}", [128, T], BF16) for d in range(16)]
                with ExitStack() as S1a:
                    xfq = [sb(S1a, f"xfq{d}", [128, TL], BF16) for d in range(16)]
                    wqb = [sb(S1a, f"wqb{i}", [128, C], BF16) for i in range(2)]
                    pq = [psm(S1a, f"pq{i}", [128, 512]) for i in range(2)]

                    SP0, SA0, SG0 = SP.n, SA.n, SG.n
                    qmm = lambda m: SP0 + m + 1
                    qev = lambda m: SA0 + m + 1
                    bo2_ld = SG0 + 16
                    xfq_ld = lambda d: SG0 + 16 + 16 * (d + 1)
                    ld_seq = []
                    for h in range(2):
                        ld_seq.append(("wq", h))
                    for h in range(2, H):
                        ld_seq.append(("wq", h))
                        ld_seq.append(("xf", h - 2))
                    for d in range(14, 16):
                        ld_seq.append(("xf", d))
                    base = SG0 + 16 + 256
                    wq_ld, xf_ld = {}, {}
                    for i, (kind, idx) in enumerate(ld_seq):
                        (wq_ld if kind == "wq" else xf_ld)[idx] = base + 16 * (i + 1)

                    with nc.Block() as blk:
                        @blk.gpsimd
                        def _(g):
                            g.memset(ones2[:], 1.0)
                            g.memset(onesf[:], 1.0)
                            gdma(g, bo2_s[:], D["bo2"])
                            for d in range(16):
                                gdma(g, xfq[d][:], D["xTq"][d * 128:(d + 1) * 128, :])
                            for kind, idx in ld_seq:
                                if kind == "wq":
                                    h = idx
                                    if h >= 2:
                                        g.wait_ge(SP.sem, qmm((h - 2) * 2 + 1))
                                    gdma(g, wqb[h % 2][:],
                                         D["wq"][h * 128:(h + 1) * 128, :])
                                else:
                                    gdma(g, x_fm[idx][:],
                                         D["xT"][idx * 128:(idx + 1) * 128, :])
                            SG.expect(base + 16 * len(ld_seq))

                        @blk.tensor
                        def _(t):
                            t.wait_ge(SG.sem, xfq_ld(15))
                            for h in range(H):
                                t.wait_ge(SG.sem, wq_ld[h])
                                for tc in range(2):
                                    m = h * 2 + tc
                                    if m >= 2:
                                        t.wait_ge(SA.sem, qev(m - 2))
                                    for c in range(16):
                                        ins = t.matmul(
                                            pq[m % 2][:],
                                            wqb[h % 2][:, c * 128:(c + 1) * 128],
                                            xfq[c][:, tc * 512:(tc + 1) * 512],
                                            start=(c == 0), stop=(c == 15))
                                    SP.inc(ins)
                                    SP.expect(qmm(m))

                        @blk.scalar
                        def _(sc):
                            for m in range(32):
                                h, tc = m // 2, m % 2
                                sc.wait_ge(SP.sem, qmm(m))
                                ins = sc.mul(q_fm[h][:, tc * 512:(tc + 1) * 512],
                                             pq[m % 2][:], SC)
                                SA.inc(ins)
                                SA.expect(qev(m))

                with ExitStack() as S1b:
                    wkvk = sb(S1b, "wkvk_s", [128, 16 * 512], BF16)
                    wkvv = sb(S1b, "wkvv_s", [128, 16 * 512], BF16)
                    pk = [psm(S1b, f"pk{i}", [128, 512]) for i in range(2)]
                    SP0, SA0, SG0 = SP.n, SA.n, SG.n
                    kmm = lambda m: SP0 + m + 1
                    kev = lambda m: SA0 + m + 1
                    with nc.Block() as blk:
                        @blk.gpsimd
                        def _(g):
                            gdma(g, wkvk[:], D["wkvk"])
                            gdma(g, wkvv[:], D["wkvv"])

                        @blk.tensor
                        def _(t):
                            t.wait_ge(SG.sem, SG.n)
                            m = 0
                            for g_ in range(KVH):
                                for tc in range(4):
                                    if m >= 2:
                                        t.wait_ge(SA.sem, kev(m - 2))
                                    for c in range(16):
                                        ins = t.matmul(
                                            pk[m % 2][:],
                                            wkvk[:, c * 512 + g_ * 128:
                                                 c * 512 + (g_ + 1) * 128],
                                            x_fm[c][:, tc * 512:(tc + 1) * 512],
                                            start=(c == 0), stop=(c == 15))
                                    SP.inc(ins); m += 1
                            for j in range(NT):
                                if m >= 2:
                                    t.wait_ge(SA.sem, kev(m - 2))
                                for c in range(16):
                                    ins = t.matmul(
                                        pk[m % 2][:],
                                        x_fm[c][:, j * 128:(j + 1) * 128],
                                        wkvv[:, c * 512:(c + 1) * 512],
                                        start=(c == 0), stop=(c == 15))
                                SP.inc(ins); m += 1
                            SP.expect(kmm(31))

                        @blk.scalar
                        def _(sc):
                            m = 0
                            for g_ in range(KVH):
                                for tc in range(4):
                                    sc.wait_ge(SP.sem, kmm(m))
                                    ins = sc.copy(k_fm[g_][:, tc * 512:(tc + 1) * 512],
                                                  pk[m % 2][:])
                                    SA.inc(ins); m += 1
                            for j in range(NT):
                                sc.wait_ge(SP.sem, kmm(m))
                                ins = sc.copy(v_tok[j][:], pk[m % 2][:])
                                SA.inc(ins); m += 1
                            SA.expect(kev(31))

            if stage == 1:
                with nc.Block() as blk:
                    @blk.sync
                    def _(s):
                        for h in range(H):
                            sdma(s, dbg["q"][h * 128:(h + 1) * 128, :], q_fm[h][:])
                        for g_ in range(KVH):
                            sdma(s, dbg["k"][g_ * 128:(g_ + 1) * 128, :], k_fm[g_][:])
                        for j in range(NT):
                            sdma(s, dbg["v"][j * 128:(j + 1) * 128, :], v_tok[j][:])
                        s.wait_ge(SS.sem, SS.n)
                return nc, D

            with ExitStack() as S2:
                slop_s = sb(S2, "slop", [4, H * 128], BF16)
                augr_s = sb(S2, "augr", [4, NL * 16 * 128], BF16)
                bias_s = sb(S2, "biasc", [128, 16], F32)
                mask_s = sb(S2, "maskt", [128, NL * 256], F32)
                E = [[sb(S2, f"E{p}{c}", [128, 512], BF16) for c in range(2)]
                     for p in range(2)]
                inv = [sb(S2, f"inv{c}", [128, 128], F32) for c in range(2)]
                psc = [[psm(S2, f"psc{p}{c}", [128, 512]) for c in range(2)]
                       for p in range(2)]
                psum_s = psm(S2, "psums", [128, 512])
                pav = psm(S2, "pav", [128, 512])

                with nc.Block() as blk:
                    @blk.gpsimd
                    def _(g):
                        gdma(g, slop_s[:], D["slopes4"])
                        gdma(g, augr_s[:], D["augrows"])
                        g.wait_ge(SG.sem, SG.n)

                    @blk.sync
                    def _(s):
                        sdma(s, bias_s[:], D["biascol"])
                        sdma(s, mask_s[:], D["masktiles"])
                        s.wait_ge(SS.sem, SS.n)

                groups = []
                for g_ in range(KVH):
                    for l in range(NL):
                        for hp in range(2):
                            nk = NKP[l]
                            ngr = (nk + 3) // 4
                            for gr in range(ngr):
                                groups.append(dict(
                                    g=g_, l=l, gr=gr, j0=gr * 4,
                                    nj=min(4, nk - gr * 4),
                                    heads=(g_ + hp * 8, g_ + hp * 8 + 4),
                                    first=(gr == 0), last=(gr == ngr - 1)))
                M = len(groups)
                seg_of, segend = {}, {}
                seg = -1
                for m, gr in enumerate(groups):
                    if gr["first"]:
                        seg += 1
                    seg_of[m] = seg
                    if gr["last"]:
                        segend[seg] = m
                NSEG = seg + 1
                SP0, SA0, SV0 = SP.n, SA.n, SV.n
                scores_done, sumsav_done = {}, {}
                pe_n = SP0
                for m in range(M):
                    pe_n += 1; scores_done[m] = pe_n
                    if m >= 1:
                        pe_n += 1; sumsav_done[m - 1] = pe_n
                pe_n += 1; sumsav_done[M - 1] = pe_n
                exp_done = {m: SA0 + m + 1 for m in range(M)}
                mask_done, evac_done = {}, {}
                dve_n = SV0
                for s_ in range(NSEG):
                    dve_n += 1; mask_done[segend[s_]] = dve_n
                    dve_n += 2; evac_done[s_] = dve_n

                with nc.Block() as blk:
                    @blk.tensor
                    def _(t):
                        def sums_av(m):
                            gr = groups[m]
                            t.wait_ge(SA.sem, exp_done[m])
                            if gr["first"] and seg_of[m] > 0:
                                t.wait_ge(SV.sem, evac_done[seg_of[m] - 1])
                            p = m % 2
                            for ci in range(2):
                                for jl in range(gr["nj"]):
                                    ins = t.matmul(
                                        psum_s[:, ci * 128:(ci + 1) * 128],
                                        onesf[:],
                                        E[p][ci][:, jl * 128:(jl + 1) * 128],
                                        start=(gr["first"] and ci == 0 and jl == 0),
                                        stop=(gr["last"] and ci == 1 and jl == gr["nj"] - 1),
                                        skip_group_check=True)
                            for ci in range(2):
                                for jl in range(gr["nj"]):
                                    j = gr["j0"] + jl
                                    ins = t.matmul(
                                        pav[:, ci * 128:(ci + 1) * 128],
                                        v_tok[j][:, gr["g"] * 128:(gr["g"] + 1) * 128],
                                        E[p][ci][:, jl * 128:(jl + 1) * 128],
                                        start=(gr["first"] and ci == 0 and jl == 0),
                                        stop=(gr["last"] and ci == 1 and jl == gr["nj"] - 1),
                                        skip_group_check=True)
                            SP.inc(ins)
                            SP.expect(sumsav_done[m])

                        for m, gr in enumerate(groups):
                            p = m % 2
                            l, g_ = gr["l"], gr["g"]
                            if m >= 2:
                                t.wait_ge(SA.sem, exp_done[m - 2])
                            for ci, h in enumerate(gr["heads"]):
                                j = gr["j0"]
                                t.matmul(
                                    psc[p][ci][:, 0:gr["nj"] * 128],
                                    slop_s[:, h * 128:(h + 1) * 128],
                                    augr_s[:, (l * 16 + j) * 128:
                                           (l * 16 + j + gr["nj"]) * 128],
                                    start=True, stop=False,
                                    skip_group_check=True)
                            for ci, h in enumerate(gr["heads"]):
                                for jl in range(gr["nj"]):
                                    j = gr["j0"] + jl
                                    ins = t.matmul(
                                        psc[p][ci][:, jl * 128:(jl + 1) * 128],
                                        k_fm[g_][:, j * 128:(j + 1) * 128],
                                        q_fm[h][:, l * 128:(l + 1) * 128],
                                        start=False, stop=(jl == gr["nj"] - 1),
                                        skip_group_check=True)
                            SP.inc(ins)
                            SP.expect(scores_done[m])
                            if m >= 1:
                                sums_av(m - 1)
                        sums_av(M - 1)

                    @blk.vector
                    def _(v):
                        for s_ in range(NSEG):
                            m = segend[s_]
                            gr = groups[m]
                            p = m % 2
                            l = gr["l"]
                            o = (gr["nj"] - 2) * 128
                            v.wait_ge(SP.sem, scores_done[m])
                            for ci in range(2):
                                ins = v.tensor_tensor(
                                    psc[p][ci][:, o:o + 256],
                                    psc[p][ci][:, o:o + 256],
                                    mask_s[:, l * 256:(l + 1) * 256], ALU.add)
                            SV.inc(ins)
                            SV.expect(mask_done[m])
                            v.wait_ge(SP.sem, sumsav_done[m])
                            for ci in range(2):
                                ins = v.reciprocal(inv[ci][:],
                                                   psum_s[:, ci * 128:(ci + 1) * 128])
                            vv = SV.inc(ins)
                            v.wait_ge(SV.sem, vv)
                            for ci, h in enumerate(gr["heads"]):
                                ins = v.tensor_tensor(
                                    oa_fm[h][:, l * 128:(l + 1) * 128],
                                    pav[:, ci * 128:(ci + 1) * 128],
                                    inv[ci][:], ALU.mult)
                            SV.inc(ins)
                            SV.expect(evac_done[s_])

                    @blk.scalar
                    def _(sc):
                        for m, gr in enumerate(groups):
                            p = m % 2
                            sc.wait_ge(SP.sem, scores_done[m])
                            if gr["last"]:
                                sc.wait_ge(SV.sem, mask_done[m])
                            for ci, h in enumerate(gr["heads"]):
                                ins = sc.activation(
                                    E[p][ci][:, 0:gr["nj"] * 128],
                                    psc[p][ci][:, 0:gr["nj"] * 128],
                                    AF.Exp, bias=bias_s[:, h:h + 1])
                            SA.inc(ins)
                            SA.expect(exp_done[m])

            if stage == 2:
                with nc.Block() as blk:
                    @blk.sync
                    def _(s):
                        for h in range(H):
                            sdma(s, dbg["attn"][h * 128:(h + 1) * 128, :], oa_fm[h][:])
                        s.wait_ge(SS.sem, SS.n)
                return nc, D

            SCB.close()
            S34 = ExitStack()
            wgb = [sb(S34, f"wgb{i}", [128, C], BF16, side="right") for i in range(3)]
            wub = [sb(S34, f"wub{i}", [128, C], BF16, side="right") for i in range(3)]
            xq_fm = [sb(S34, f"xqf{c}", [128, TL], BF16, side="right")
                     for c in range(16)]
            NPRE = 0 if os.environ.get("DEVK_NOPRE") else 3
            with ExitStack() as S3:
                x1p = [sb(S3, f"x1p{l}", [128, C], F32) for l in range(NL)]
                wob = [sb(S3, f"wob{i}", [128, 16 * 512], BF16) for i in range(2)]
                x1n = sb(S3, "x1n", [128, C], F32)
                sqs = x1n
                xqb = [sb(S3, f"xqb{i}", [128, C], BF16) for i in range(2)]
                cn = ("sum", "sq", "mean", "var", "msq", "rstd", "nb", "amax",
                      "iam", "s127", "a127", "lnv")
                col = {nm: sb(S3, "c_" + nm, [128, 1], F32) for nm in cn}
                py = [psm(S3, f"py{i}", [128, 512]) for i in range(2)]

                SP0, SA0, SV0, SS0, SG0 = SP.n, SA.n, SV.n, SS.n, SG.n
                ymm = lambda m: SP0 + m + 1
                wo_ld = lambda fk: SG0 + 16 * (fk + 1)
                xres_ld = lambda l: SS0 + 16 * (l + 1)
                x1_st = lambda l: SS0 + 128 + 32 * l + 16
                xq_st = lambda l: SS0 + 128 + 32 * l + 32
                def yev(fk, l):
                    if fk < 3:
                        return SV0 + fk * 8 + l + 1
                    return SV0 + 24 + 9 * l + 1
                VCH = lambda l, i: SV0 + 24 + 9 * l + 1 + i
                ACH = lambda l, i: SA0 + 6 * l + i
                SGW = SG0 + 64
                wldF = lambda hc, mat: SGW + 16 * (hc * 2 + mat + 1)
                XQ7 = SS0 + 128 + 32 * 7 + 32
                xqfF = lambda c, _b=XQ7: _b + 16 * (c + 1)

                with nc.Block() as blk:
                    @blk.gpsimd
                    def _(g):
                        for fk in range(4):
                            if fk >= 2:
                                g.wait_ge(SP.sem, ymm((fk - 2) * 8 + 7))
                            gdma(g, wob[fk % 2][:],
                                 D["wo"][fk * 128:(fk + 1) * 128, :])
                            assert SG.n == wo_ld(fk)
                        for hc in range(NPRE):
                            for mat, dr in ((0, "wgt"), (1, "wut")):
                                gdma(g, (wgb if mat == 0 else wub)[hc % 3][:],
                                     D[dr][hc * 128:(hc + 1) * 128, :])
                                assert SG.n == wldF(hc, mat)

                    @blk.sync
                    def _(s):
                        for l in range(NL):
                            sdma(s, x1p[l][:], D["xres"][l * 128:(l + 1) * 128, :])
                        for l in range(NL):
                            s.wait_ge(SV.sem, VCH(l, 3))
                            sdma(s, D["x1d"][l * 128:(l + 1) * 128, :], x1p[l][:])
                            s.wait_ge(SV.sem, VCH(l, 8))
                            sdma(s, D["xqd"][l * 128:(l + 1) * 128, :], xqb[l % 2][:])
                            assert SS.n == xq_st(l)
                        if not os.environ.get("DEVK_NOTR"):
                            s.wait_ge(SS.sem, xq_st(7))
                            for c in range(16):
                                sdma(s, xq_fm[c][:],
                                     D["xqd"][:, c * 128:(c + 1) * 128],
                                     transpose=True)
                                assert SS.n == xqfF(c)

                    @blk.tensor
                    def _(t):
                        for fk in range(4):
                            t.wait_ge(SG.sem, wo_ld(fk))
                            for l in range(NL):
                                m = fk * 8 + l
                                if m >= 2:
                                    fkp, lp = divmod(m - 2, 8)
                                    t.wait_ge(SV.sem, yev(fkp, lp))
                                t.matmul(py[m % 2][:], ones2[:],
                                         bo2_s[:, fk * 512:(fk + 1) * 512],
                                         start=True, stop=False,
                                         skip_group_check=True)
                                for h in range(H):
                                    ins = t.matmul(
                                        py[m % 2][:],
                                        oa_fm[h][:, l * 128:(l + 1) * 128],
                                        wob[fk % 2][:, h * 512:(h + 1) * 512],
                                        start=False, stop=(h == H - 1),
                                        skip_group_check=True)
                                SP.inc(ins)
                                SP.expect(ymm(m))

                    @blk.vector
                    def _(v):
                        def evac(fk, l):
                            m = fk * 8 + l
                            v.wait_ge(SP.sem, ymm(m))
                            if fk == 0:
                                v.wait_ge(SS.sem, xres_ld(l))
                            ins = v.tensor_tensor(
                                x1p[l][:, fk * 512:(fk + 1) * 512],
                                x1p[l][:, fk * 512:(fk + 1) * 512],
                                py[m % 2][:], ALU.add)
                            SV.inc(ins)
                            SV.expect(yev(fk, l))
                        for fk in range(3):
                            for l in range(NL):
                                evac(fk, l)
                        for l in range(NL):
                            evac(3, l)
                            v.wait_ge(SA.sem, ACH(l, 1))
                            ins = v.reduce_sum(col["sum"][:], x1p[l][:], AX.X)
                            vv = SV.inc(ins)
                            v.wait_ge(SV.sem, vv)
                            v.tensor_scalar_mul(col["mean"][:], col["sum"][:], 1.0 / C)
                            v.tensor_scalar_mul(col["var"][:], col["sq"][:], 1.0 / C)
                            v.tensor_tensor(col["msq"][:], col["mean"][:],
                                            col["mean"][:], ALU.mult)
                            v.tensor_tensor(col["var"][:], col["var"][:],
                                            col["msq"][:], ALU.subtract)
                            ins = v.tensor_scalar_add(col["var"][:], col["var"][:], EPS_LN)
                            SV.inc(ins)
                            v.wait_ge(SA.sem, ACH(l, 3))
                            v.scalar_tensor_tensor(col["nb"][:], col["mean"][:], -1.0,
                                                   col["rstd"][:], ALU.mult, ALU.mult)
                            ins = v.tensor_scalar(x1p[l][:], x1p[l][:], col["rstd"][:],
                                                  col["nb"][:], ALU.mult, ALU.add)
                            SV.inc(ins)
                            v.wait_ge(SA.sem, ACH(l, 4))
                            ins = v.reduce_sum(col["sum"][:], x1p[l][:], AX.X)
                            vv = SV.inc(ins)
                            v.wait_ge(SV.sem, vv)
                            v.tensor_scalar_mul(col["mean"][:], col["sum"][:], 1.0 / C)
                            v.tensor_scalar_mul(col["var"][:], col["sq"][:], 1.0 / C)
                            v.tensor_tensor(col["msq"][:], col["mean"][:],
                                            col["mean"][:], ALU.mult)
                            v.tensor_tensor(col["var"][:], col["var"][:],
                                            col["msq"][:], ALU.subtract)
                            ins = v.tensor_scalar_add(col["var"][:], col["var"][:], EPS_RMS)
                            SV.inc(ins)
                            v.wait_ge(SA.sem, ACH(l, 6))
                            v.scalar_tensor_tensor(col["nb"][:], col["mean"][:], -1.0,
                                                   col["rstd"][:], ALU.mult, ALU.mult)
                            v.tensor_scalar(x1n[:], x1p[l][:], col["rstd"][:],
                                            col["nb"][:], ALU.mult, ALU.add)
                            ins = v.tensor_reduce(col["amax"][:], x1n[:], AX.X,
                                                  ALU.max, apply_absolute_value=True)
                            vv = SV.inc(ins)
                            v.wait_ge(SV.sem, vv)
                            ins = v.reciprocal(col["iam"][:], col["amax"][:])
                            vv = SV.inc(ins)
                            v.wait_ge(SV.sem, vv)
                            v.tensor_scalar_mul(col["s127"][:], col["iam"][:], 127.0)
                            v.tensor_scalar_mul(col["a127"][:], col["amax"][:], 1.0 / 127.0)
                            if l >= 2:
                                v.wait_ge(SS.sem, xq_st(l - 2))
                            v.tensor_scalar(x1n[:], x1n[:], col["s127"][:], MAGIC,
                                            ALU.mult, ALU.add)
                            v.tensor_scalar_sub(x1n[:], x1n[:], MAGIC)
                            ins = v.tensor_scalar_mul(xqb[l % 2][:], x1n[:], col["a127"][:])
                            SV.inc(ins)
                            SV.expect(VCH(l, 8))

                    @blk.scalar
                    def _(sc):
                        for l in range(NL):
                            sc.wait_ge(SV.sem, yev(3, l))
                            ins = sc.activation(sqs[:], x1p[l][:], AF.Square,
                                                accum_out=col["sq"][:])
                            SA.inc(ins)
                            sc.wait_ge(SV.sem, VCH(l, 2))
                            ins = sc.activation(col["lnv"][:], col["var"][:], AF.Ln)
                            aa = SA.inc(ins)
                            sc.wait_ge(SA.sem, aa)
                            ins = sc.activation(col["rstd"][:], col["lnv"][:],
                                                AF.Exp, scale=-0.5)
                            SA.inc(ins)
                            sc.wait_ge(SV.sem, VCH(l, 3))
                            ins = sc.activation(sqs[:], x1p[l][:], AF.Square,
                                                accum_out=col["sq"][:])
                            SA.inc(ins)
                            sc.wait_ge(SV.sem, VCH(l, 5))
                            ins = sc.activation(col["lnv"][:], col["var"][:], AF.Ln)
                            aa = SA.inc(ins)
                            sc.wait_ge(SA.sem, aa)
                            ins = sc.activation(col["rstd"][:], col["lnv"][:],
                                                AF.Exp, scale=-0.5)
                            SA.inc(ins)
                            SA.expect(ACH(l, 6))

            if stage == 3:
                with nc.Block() as blk:
                    @blk.sync
                    def _(s):
                        s.wait_ge(SS.sem, SS.n)
                return nc, D

        with ExitStack() as S4:
            p_fm = [sb(S4, f"pfm{hc}", [128, TL], BF16) for hc in range(64)]
            S4a = S4.enter_context(ExitStack())
            gsg = [sb(S4a, f"gsg{i}", [128, 512], BF16) for i in range(4)]
            pgu = [psm(S4a, f"pgu{i}", [128, 512]) for i in range(8)]
            SP0, SA0, SV0, SS0, SG0 = SP.n, SA.n, SV.n, SS.n, SG.n
            wld = wldF
            sgn_done = lambda hc, mat: SA0 + hc * 4 + mat + 1
            silu_done = lambda hc, th: SA0 + hc * 4 + 2 + th + 1
            gu = lambda hc, mat, th: SP0 + hc * 4 + mat * 2 + th + 1
            pd = lambda hc, th: SV0 + hc * 2 + th + 1

            with nc.Block() as blk:
                if os.environ.get("DEVK_NOTR"):
                    @blk.sync
                    def _(s):
                        s.wait_ge(SS.sem, xqfF(0) - 16)
                        for c in range(16):
                            sdma(s, xq_fm[c][:],
                                 D["xqd"][:, c * 128:(c + 1) * 128],
                                 transpose=True)
                            assert SS.n == xqfF(c)

                @blk.gpsimd
                def _(g):
                    for hc in range(NPRE, 64):
                        for mat, dr in ((0, "wgt"), (1, "wut")):
                            g.wait_ge(SP.sem, gu(hc - 3, mat, 1))
                            gdma(g, (wgb if mat == 0 else wub)[hc % 3][:],
                                 D[dr][hc * 128:(hc + 1) * 128, :])
                            assert SG.n == wld(hc, mat)

                @blk.scalar
                def _(sc):
                    for hc in range(64):
                        for mat in range(2):
                            wb = (wgb if mat == 0 else wub)[hc % 3]
                            sc.wait_ge(SG.sem, wld(hc, mat))
                            if hc >= 3:
                                sc.wait_ge(SP.sem, gu(hc - 3, mat, 1))
                            ins = sc.activation(wb[:], wb[:], AF.Sign)
                            SA.inc(ins)
                            SA.expect(sgn_done(hc, mat))
                        for th in range(2):
                            sc.wait_ge(SP.sem, gu(hc, 0, th))
                            if hc >= 2:
                                sc.wait_ge(SV.sem, pd(hc - 2, th))
                            bank = 0 * 4 + th * 2 + hc % 2
                            ins = sc.activation(gsg[th * 2 + hc % 2][:],
                                                pgu[bank][:], AF.Silu)
                            SA.inc(ins)
                            SA.expect(silu_done(hc, th))

                @blk.tensor
                def _(t):
                    t.wait_ge(SS.sem, xqfF(15))
                    for hc in range(64):
                        for mat in range(2):
                            wb = (wgb if mat == 0 else wub)[hc % 3]
                            t.wait_ge(SA.sem, sgn_done(hc, mat))
                            for th in range(2):
                                bank = mat * 4 + th * 2 + hc % 2
                                if hc >= 2:
                                    t.wait_ge(SV.sem, pd(hc - 2, th))
                                for c in range(16):
                                    ins = t.matmul(
                                        pgu[bank][:],
                                        wb[:, c * 128:(c + 1) * 128],
                                        xq_fm[c][:, th * 512:(th + 1) * 512],
                                        start=(c == 0), stop=(c == 15))
                                SP.inc(ins)
                                SP.expect(gu(hc, mat, th))

                @blk.vector
                def _(v):
                    for hc in range(64):
                        for th in range(2):
                            v.wait_ge(SA.sem, silu_done(hc, th))
                            v.wait_ge(SP.sem, gu(hc, 1, th))
                            ubank = 1 * 4 + th * 2 + hc % 2
                            ins = v.tensor_tensor(
                                p_fm[hc][:, th * 512:(th + 1) * 512],
                                gsg[th * 2 + hc % 2][:], pgu[ubank][:], ALU.mult)
                            SV.inc(ins)
                            SV.expect(pd(hc, th))

            if stage == 34:
                with nc.Block() as blk:
                    @blk.sync
                    def _(s):
                        for hc in range(64):
                            sdma(s, dbg["p"][hc * 128:(hc + 1) * 128, :], p_fm[hc][:])
                        s.wait_ge(SS.sem, SS.n)
                return nc, D

            S4a.close()
            S34.close()
            with ExitStack() as S5:
                wdb = [sb(S5, f"wdb{i}", [128, 64 * 128], BF16) for i in range(2)]
                yst = [sb(S5, f"yst{i}", [128, 128], F32) for i in range(4)]
                pdn = [psm(S5, f"pdn{i}", [128, 512]) for i in range(4)]
                SP0, SA0, SV0, SS0, SG0 = SP.n, SA.n, SV.n, SS.n, SG.n
                wdld = lambda sx: SG0 + 16 * (sx + 1)
                wsgn = lambda sx: SA0 + sx + 1
                dn = lambda m: SP0 + m + 1
                yev5 = lambda m: SV0 + m + 1
                yst5 = lambda m: SS0 + 16 * (m + 1)

                with nc.Block() as blk:
                    @blk.gpsimd
                    def _(g):
                        for sx in range(16):
                            if sx >= 2:
                                g.wait_ge(SP.sem, dn((sx - 2) * 8 + 7))
                            gdma(g, wdb[sx % 2][:],
                                 D["wdt"][sx * 128:(sx + 1) * 128, :])
                            assert SG.n == wdld(sx)

                    @blk.scalar
                    def _(sc):
                        for sx in range(16):
                            sc.wait_ge(SG.sem, wdld(sx))
                            if sx >= 2:
                                sc.wait_ge(SP.sem, dn((sx - 2) * 8 + 7))
                            ins = sc.activation(wdb[sx % 2][:], wdb[sx % 2][:], AF.Sign)
                            SA.inc(ins)
                            SA.expect(wsgn(sx))

                    @blk.tensor
                    def _(t):
                        for sx in range(16):
                            t.wait_ge(SA.sem, wsgn(sx))
                            for l in range(NL):
                                m = sx * 8 + l
                                if m >= 4:
                                    t.wait_ge(SV.sem, yev5(m - 4))
                                for hc in range(64):
                                    ins = t.matmul(
                                        pdn[m % 4][:, 0:128],
                                        p_fm[hc][:, l * 128:(l + 1) * 128],
                                        wdb[sx % 2][:, hc * 128:(hc + 1) * 128],
                                        start=(hc == 0), stop=(hc == 63))
                                SP.inc(ins)
                                SP.expect(dn(m))

                    @blk.vector
                    def _(v):
                        for m in range(128):
                            v.wait_ge(SP.sem, dn(m))
                            if m >= 4:
                                v.wait_ge(SS.sem, yst5(m - 4))
                            ins = v.tensor_copy(yst[m % 4][:], pdn[m % 4][:, 0:128])
                            SV.inc(ins)
                            SV.expect(yev5(m))

                    @blk.sync
                    def _(s):
                        for m in range(128):
                            sx, l = divmod(m, 8)
                            s.wait_ge(SV.sem, yev5(m))
                            sdma(s, D["y2d"][l * 128:(l + 1) * 128,
                                             sx * 128:(sx + 1) * 128], yst[m % 4][:])
                            assert SS.n == yst5(m)

        if stage == 45:
            with nc.Block() as blk:
                @blk.sync
                def _(s):
                    s.wait_ge(SS.sem, SS.n)
            return nc, D

        with ExitStack() as S6:
            x2p = [sb(S6, f"x2p{i}", [128, C], F32) for i in range(2)]
            y2t = [sb(S6, f"y2t{i}", [128, C], F32) for i in range(2)]
            outb = [sb(S6, f"outb{i}", [128, C], F32) for i in range(2)]
            sqs6 = sb(S6, "sqs6", [128, C], F32)
            col6 = {nm: sb(S6, "d_" + nm, [128, 1], F32) for nm in
                    ("sum", "sq", "mean", "var", "msq", "rstd", "nb", "lnv")}
            SP0, SA0, SV0, SS0 = SP.n, SA.n, SV.n, SS.n
            _seq = [("L", 0), ("L", 1)]
            for l in range(2, NL):
                _seq += [("S", l - 2), ("L", l)]
            _seq += [("S", NL - 2), ("S", NL - 1)]
            _ld6, _st6 = {}, {}
            _n = SS0
            for kind, l in _seq:
                _n += 32 if kind == "L" else 16
                (_ld6 if kind == "L" else _st6)[l] = _n
            ld6 = lambda l: _ld6[l]
            st6 = lambda l: _st6[l]
            V6_ = lambda l, i: SV0 + 4 * l + i
            A6_ = lambda l, i: SA0 + 3 * l + i

            with nc.Block() as blk:
                @blk.sync
                def _(s):
                    s.wait_ge(SS.sem, SS0)
                    for kind, l in _seq:
                        if kind == "L":
                            if l >= 2:
                                s.wait_ge(SV.sem, V6_(l - 2, 4))
                            sdma(s, x2p[l % 2][:], D["x1d"][l * 128:(l + 1) * 128, :])
                            sdma(s, y2t[l % 2][:], D["y2d"][l * 128:(l + 1) * 128, :])
                            assert SS.n == ld6(l)
                        else:
                            s.wait_ge(SV.sem, V6_(l, 4))
                            sdma(s, D["y"][l * 128:(l + 1) * 128, :], outb[l % 2][:])
                            assert SS.n == st6(l)
                    s.wait_ge(SS.sem, SS.n)

                @blk.vector
                def _(v):
                    for l in range(NL):
                        v.wait_ge(SS.sem, ld6(l))
                        ins = v.tensor_tensor(x2p[l % 2][:], x2p[l % 2][:],
                                              y2t[l % 2][:], ALU.add)
                        SV.inc(ins)
                        v.wait_ge(SA.sem, A6_(l, 1))
                        ins = v.reduce_sum(col6["sum"][:], x2p[l % 2][:], AX.X)
                        vv = SV.inc(ins)
                        v.wait_ge(SV.sem, vv)
                        v.tensor_scalar_mul(col6["mean"][:], col6["sum"][:], 1.0 / C)
                        v.tensor_scalar_mul(col6["var"][:], col6["sq"][:], 1.0 / C)
                        v.tensor_tensor(col6["msq"][:], col6["mean"][:],
                                        col6["mean"][:], ALU.mult)
                        v.tensor_tensor(col6["var"][:], col6["var"][:],
                                        col6["msq"][:], ALU.subtract)
                        ins = v.tensor_scalar_add(col6["var"][:], col6["var"][:], EPS_LN)
                        SV.inc(ins)
                        v.wait_ge(SA.sem, A6_(l, 3))
                        v.scalar_tensor_tensor(col6["nb"][:], col6["mean"][:], -1.0,
                                               col6["rstd"][:], ALU.mult, ALU.mult)
                        if l >= 2:
                            v.wait_ge(SS.sem, st6(l - 2))
                        ins = v.tensor_scalar(outb[l % 2][:], x2p[l % 2][:],
                                              col6["rstd"][:], col6["nb"][:],
                                              ALU.mult, ALU.add)
                        SV.inc(ins)
                        SV.expect(V6_(l, 4))

                @blk.scalar
                def _(sc):
                    for l in range(NL):
                        sc.wait_ge(SV.sem, V6_(l, 1))
                        ins = sc.activation(sqs6[:], x2p[l % 2][:], AF.Square,
                                            accum_out=col6["sq"][:])
                        SA.inc(ins)
                        sc.wait_ge(SV.sem, V6_(l, 3))
                        ins = sc.activation(col6["lnv"][:], col6["var"][:], AF.Ln)
                        aa = SA.inc(ins)
                        sc.wait_ge(SA.sem, aa)
                        ins = sc.activation(col6["rstd"][:], col6["lnv"][:],
                                            AF.Exp, scale=-0.5)
                        SA.inc(ins)
                        SA.expect(A6_(l, 3))

    return nc, D


def kernel(**inputs):
    nc, _ = build(stage=5)
    in_maps = [stage_core(inputs, c) for c in range(8)]
    res = run_bass_kernel_spmd(nc, in_maps, core_ids=list(range(8)))
    out = np.empty((B, T, C), dtype=np.float32)
    for c in range(8):
        b, par = c // 2, c % 2
        y = np.asarray(res.results[c]["y"])
        for li, g in enumerate(TILES[par]):
            out[b, g * 128:(g + 1) * 128, :] = y[li * 128:(li + 1) * 128, :]
    return out


# version 19
# speedup vs baseline: 1.1056x; 1.1056x over previous
import os
import numpy as np
import ml_dtypes
import concourse.bass as bass
import concourse.mybir as mybir
from concourse.bass_utils import run_bass_kernel_spmd
from contextlib import ExitStack

F32 = mybir.dt.float32
BF16 = mybir.dt.bfloat16
AF = mybir.ActivationFunctionType
ALU = mybir.AluOpType
AX = mybir.AxisListType
bfd = ml_dtypes.bfloat16

B, T, C = 4, 2048, 2048
H, KVH, HD = 16, 4, 128
NT, NL, TL = 16, 8, 1024
HF = 8192
SC = 1.0 / np.sqrt(128.0)
MAGIC = 1.5 * 2.0 ** 23
EPS_LN = 1e-5
EPS_RMS = 1e-8
NEG = -1.0e30
TILES = {0: [0, 3, 4, 7, 8, 11, 12, 15], 1: [1, 2, 5, 6, 9, 10, 13, 14]}
NKP = [2 * (l + 1) for l in range(NL)]


def _hilo(a):
    hi = a.astype(bfd).astype(np.float32)
    lo = (a - hi).astype(bfd).astype(np.float32)
    return hi, lo


def _slopes():
    start = 2.0 ** (-(2.0 ** -(np.log2(H) - 3.0)))
    return (start ** (np.arange(H) + 1)).astype(np.float64)


def _tile_w(w, fsz):
    K, F = w.shape
    a = w.reshape(K // 128, 128, F // fsz, fsz)
    return np.ascontiguousarray(a.transpose(2, 1, 0, 3)).reshape(
        F // fsz * 128, K // 128 * fsz)


def stage_core(inp, core):
    b, par = core // 2, core % 2
    tiles = TILES[par]
    x = np.asarray(inp["L0_attn_x"][b], dtype=np.float32)
    own = np.concatenate([np.arange(g * 128, (g + 1) * 128) for g in tiles])
    sl = _slopes()
    s_hi, s_lo = _hilo(sl.astype(np.float32))
    slopes4 = np.zeros((4, H * 128), dtype=np.float32)
    for h in range(H):
        slopes4[0, h * 128:(h + 1) * 128] = s_hi[h]
        slopes4[1, h * 128:(h + 1) * 128] = s_hi[h]
        slopes4[2, h * 128:(h + 1) * 128] = s_lo[h]
        slopes4[3, h * 128:(h + 1) * 128] = s_lo[h]
    biascol = (np.asarray(sl)[None, :] * np.arange(128)[:, None]).astype(np.float32)
    augrows = np.zeros((4, NL * 16 * 128), dtype=np.float32)
    for l in range(NL):
        G = tiles[l]
        for j in range(NKP[l]):
            off = (l * 16 + j) * 128
            if j <= G:
                r = (128.0 * (j - G) - np.arange(128)).astype(np.float32)
                hi, lo = _hilo(r)
            else:
                hi = np.full(128, NEG, dtype=np.float32)
                lo = np.full(128, NEG, dtype=np.float32)
            augrows[0, off:off + 128] = hi
            augrows[1, off:off + 128] = lo
            augrows[2, off:off + 128] = hi
            augrows[3, off:off + 128] = lo
    masktiles = np.zeros((128, NL * 256), dtype=np.float32)
    tri = np.where(np.arange(128)[:, None] > np.arange(128)[None, :], NEG, 0.0
                   ).astype(np.float32)
    for l in range(NL):
        G, nk = tiles[l], NKP[l]
        for slot, j in enumerate((nk - 2, nk - 1)):
            if j == G:
                masktiles[:, l * 256 + slot * 128: l * 256 + (slot + 1) * 128] = tri
    bo_hi, bo_lo = _hilo(np.asarray(inp["L0_attn_bo"], dtype=np.float32))
    return {
        "xT": np.ascontiguousarray(x.T),
        "xTq": np.ascontiguousarray(x.T[:, own]),
        "xres": np.ascontiguousarray(x[own]),
        "wq": _tile_w(np.asarray(inp["L0_attn_Wq"], np.float32), 128),
        "wkvk": _tile_w(np.asarray(inp["L0_attn_Wkv"], np.float32)[:, :512], 512),
        "wkvv": _tile_w(np.asarray(inp["L0_attn_Wkv"], np.float32)[:, 512:], 512),
        "wo": _tile_w(np.asarray(inp["L0_attn_Wo"], np.float32), 512),
        "bo2": np.stack([bo_hi, bo_lo]),
        "slopes4": slopes4, "biascol": biascol,
        "augrows": augrows, "masktiles": masktiles,
        "wgt": _tile_w(np.asarray(inp["L0_ffn_W_g"], np.float32).T, 128),
        "wut": _tile_w(np.asarray(inp["L0_ffn_W_u"], np.float32).T, 128),
        "wdt": _tile_w(np.asarray(inp["L0_ffn_W_d"], np.float32).T, 128),
    }


def build(stage=5):
    nc = bass.Bass("TRN2", target_bir_lowering=False)
    D = {}
    for nm, shape, dt in (
            ("xT", [C, T], F32), ("xTq", [C, TL], F32), ("xres", [TL, C], F32),
            ("wq", [C, C], F32), ("wkvk", [128, 16 * 512], F32),
            ("wkvv", [128, 16 * 512], F32), ("wo", [512, 16 * 512], F32),
            ("bo2", [2, C], F32), ("slopes4", [4, H * 128], F32),
            ("biascol", [128, 16], F32), ("augrows", [4, NL * 16 * 128], F32),
            ("masktiles", [128, NL * 256], F32),
            ("wgt", [HF, C], F32), ("wut", [HF, C], F32),
            ("wdt", [C, HF], F32)):
        D[nm] = nc.dram_tensor(nm, shape, dt, kind="ExternalInput").ap()
    D["y"] = nc.dram_tensor("y", [TL, C], F32, kind="ExternalOutput").ap()
    D["wob16"] = nc.dram_tensor("wob16", [512, 16 * 512], BF16, kind="Internal").ap()
    sk3 = "ExternalOutput" if stage in (3, 4) else "Internal"
    sk4 = "ExternalOutput" if stage in (4, 45) else "Internal"
    D["x1d"] = nc.dram_tensor("x1d", [TL, C], F32, kind=sk3).ap()
    D["xqd"] = nc.dram_tensor("xqd", [TL, C], BF16, kind=sk3).ap()
    D["y2d"] = nc.dram_tensor("y2d", [TL, C], F32, kind=sk4).ap()
    dbg = {}
    if stage == 1:
        dbg["q"] = nc.dram_tensor("dbg_q", [H * 128, TL], BF16, kind="ExternalOutput").ap()
        dbg["k"] = nc.dram_tensor("dbg_k", [KVH * 128, T], BF16, kind="ExternalOutput").ap()
        dbg["v"] = nc.dram_tensor("dbg_v", [T, 512], BF16, kind="ExternalOutput").ap()
    if stage == 2:
        dbg["attn"] = nc.dram_tensor("dbg_attn", [H * 128, TL], BF16, kind="ExternalOutput").ap()
    if stage == 34:
        dbg["p"] = nc.dram_tensor("dbg_p", [64 * 128, TL], BF16, kind="ExternalOutput").ap()

    class Ct:
        def __init__(self, name):
            self.sem = nc.semaphore(name).__enter__()
            self.n = 0
        def inc(self, ins, k=1):
            ins.then_inc(self.sem, k)
            self.n += k
            return self.n
        def expect(self, v):
            assert self.n == v, (self.n, v)

    SG, SS, SP, SA, SV = Ct("SG"), Ct("SS"), Ct("SP"), Ct("SA"), Ct("SV")

    def gdma(g, dst, src):
        return SG.inc(g.dma_start(dst, src), 16)

    def sdma(s, dst, src, **kw):
        return SS.inc(s.dma_start(dst, src, **kw), 16)

    def gdma3(g, dst, src, inner):
        return SG.inc(g.dma_start(
            dst[:].rearrange("p (c f) -> p c f", f=inner),
            src.rearrange("(c p) f -> p c f", p=128)), 16)

    with ExitStack() as SCO:
        def sb(st, name, shape, dt, side=None):
            return st.enter_context(nc.sbuf_tensor(name, shape, dt, side=side))
        def psm(st, name, shape):
            return st.enter_context(nc.psum_tensor(name, shape, F32))
        ones2p = sb(SCO, "ones2p", [128, 128], BF16)
        onesf = sb(SCO, "onesf", [128, 128], BF16)

        with ExitStack() as SCA:
            bo2_s = sb(SCA, "bo2_s", [128, C], BF16)
            oa_fm = [sb(SCA, f"oafm{h}", [128, TL], BF16) for h in range(H)]
            SCB = SCA.enter_context(ExitStack())
            q_fm = [sb(SCB, f"qfm{h}", [128, TL], BF16) for h in range(H)]
            k_fm = [sb(SCB, f"kfm{g}", [128, T], BF16) for g in range(KVH)]
            v_tok = [sb(SCB, f"vtok{j}", [128, 512], BF16) for j in range(NT)]

            with ExitStack() as S1:
                x_fm = [sb(S1, f"xfm{d}", [128, T], BF16) for d in range(16)]
                with ExitStack() as S1a:
                    xfq = [sb(S1a, f"xfq{d}", [128, TL], BF16) for d in range(16)]
                    wqb = [sb(S1a, f"wqb{i}", [128, C], BF16) for i in range(2)]
                    pq = [psm(S1a, f"pq{i}", [128, 512]) for i in range(2)]

                    SP0, SA0, SG0 = SP.n, SA.n, SG.n
                    qmm = lambda m: SP0 + m + 1
                    qev = lambda m: SA0 + m + 1
                    bo2_ld = SG0 + 16
                    xfq_ld = lambda d: SG0 + 16 + 16 * (d + 1)
                    ld_seq = []
                    for h in range(2):
                        ld_seq.append(("wq", h))
                    for h in range(2, H):
                        ld_seq.append(("wq", h))
                        ld_seq.append(("xf", h - 2))
                    for d in range(14, 16):
                        ld_seq.append(("xf", d))
                    base = SG0 + 16 + 256
                    wq_ld, xf_ld = {}, {}
                    for i, (kind, idx) in enumerate(ld_seq):
                        (wq_ld if kind == "wq" else xf_ld)[idx] = base + 16 * (i + 1)

                    with nc.Block() as blk:
                        @blk.gpsimd
                        def _(g):
                            g.memset(ones2p[:], 0.0)
                            g.memset(ones2p[0:2, :], 1.0)
                            g.memset(onesf[:], 1.0)
                            g.memset(bo2_s[:], 0.0)
                            gdma(g, bo2_s[0:2, :], D["bo2"])
                            for d in range(16):
                                gdma(g, xfq[d][:], D["xTq"][d * 128:(d + 1) * 128, :])
                            for kind, idx in ld_seq:
                                if kind == "wq":
                                    h = idx
                                    if h >= 2:
                                        g.wait_ge(SP.sem, qmm((h - 2) * 2 + 1))
                                    gdma(g, wqb[h % 2][:],
                                         D["wq"][h * 128:(h + 1) * 128, :])
                                else:
                                    gdma(g, x_fm[idx][:],
                                         D["xT"][idx * 128:(idx + 1) * 128, :])
                            SG.expect(base + 16 * len(ld_seq))

                        @blk.tensor
                        def _(t):
                            t.wait_ge(SG.sem, xfq_ld(15))
                            for h in range(H):
                                t.wait_ge(SG.sem, wq_ld[h])
                                for tc in range(2):
                                    m = h * 2 + tc
                                    if m >= 2:
                                        t.wait_ge(SA.sem, qev(m - 2))
                                    for c in range(16):
                                        ins = t.matmul(
                                            pq[m % 2][:],
                                            wqb[h % 2][:, c * 128:(c + 1) * 128],
                                            xfq[c][:, tc * 512:(tc + 1) * 512],
                                            start=(c == 0), stop=(c == 15))
                                    SP.inc(ins)
                                    SP.expect(qmm(m))

                        @blk.scalar
                        def _(sc):
                            for m in range(32):
                                h, tc = m // 2, m % 2
                                sc.wait_ge(SP.sem, qmm(m))
                                ins = sc.mul(q_fm[h][:, tc * 512:(tc + 1) * 512],
                                             pq[m % 2][:], SC)
                                SA.inc(ins)
                                SA.expect(qev(m))

                with ExitStack() as S1b:
                    wkvk = sb(S1b, "wkvk_s", [128, 16 * 512], BF16)
                    wkvv = sb(S1b, "wkvv_s", [128, 16 * 512], BF16)
                    pk = [psm(S1b, f"pk{i}", [128, 512]) for i in range(2)]
                    SP0, SA0, SG0 = SP.n, SA.n, SG.n
                    kmm = lambda m: SP0 + m + 1
                    kev = lambda m: SA0 + m + 1
                    with nc.Block() as blk:
                        @blk.gpsimd
                        def _(g):
                            gdma(g, wkvk[:], D["wkvk"])
                            gdma(g, wkvv[:], D["wkvv"])

                        @blk.tensor
                        def _(t):
                            t.wait_ge(SG.sem, SG.n)
                            m = 0
                            for g_ in range(KVH):
                                for tc in range(4):
                                    if m >= 2:
                                        t.wait_ge(SA.sem, kev(m - 2))
                                    for c in range(16):
                                        ins = t.matmul(
                                            pk[m % 2][:],
                                            wkvk[:, c * 512 + g_ * 128:
                                                 c * 512 + (g_ + 1) * 128],
                                            x_fm[c][:, tc * 512:(tc + 1) * 512],
                                            start=(c == 0), stop=(c == 15))
                                    SP.inc(ins); m += 1
                            for j in range(NT):
                                if m >= 2:
                                    t.wait_ge(SA.sem, kev(m - 2))
                                for c in range(16):
                                    ins = t.matmul(
                                        pk[m % 2][:],
                                        x_fm[c][:, j * 128:(j + 1) * 128],
                                        wkvv[:, c * 512:(c + 1) * 512],
                                        start=(c == 0), stop=(c == 15))
                                SP.inc(ins); m += 1
                            SP.expect(kmm(31))

                        @blk.scalar
                        def _(sc):
                            m = 0
                            for g_ in range(KVH):
                                for tc in range(4):
                                    sc.wait_ge(SP.sem, kmm(m))
                                    ins = sc.copy(k_fm[g_][:, tc * 512:(tc + 1) * 512],
                                                  pk[m % 2][:])
                                    SA.inc(ins); m += 1
                            for j in range(NT):
                                sc.wait_ge(SP.sem, kmm(m))
                                ins = sc.copy(v_tok[j][:], pk[m % 2][:])
                                SA.inc(ins); m += 1
                            SA.expect(kev(31))

            if stage == 1:
                with nc.Block() as blk:
                    @blk.sync
                    def _(s):
                        for h in range(H):
                            sdma(s, dbg["q"][h * 128:(h + 1) * 128, :], q_fm[h][:])
                        for g_ in range(KVH):
                            sdma(s, dbg["k"][g_ * 128:(g_ + 1) * 128, :], k_fm[g_][:])
                        for j in range(NT):
                            sdma(s, dbg["v"][j * 128:(j + 1) * 128, :], v_tok[j][:])
                        s.wait_ge(SS.sem, SS.n)
                return nc, D

            with ExitStack() as S2:
                slop_s = sb(S2, "slop", [128, H * 128], BF16)
                augr_s = sb(S2, "augr", [128, NL * 16 * 128], BF16)
                bias_s = sb(S2, "biasc", [128, 16], F32)
                mask_s = sb(S2, "maskt", [128, NL * 256], F32)
                E = [[sb(S2, f"E{p}{c}", [128, 512], BF16) for c in range(2)]
                     for p in range(3)]
                inv = [sb(S2, f"inv{c}", [128, 128], F32) for c in range(2)]
                psc = [[psm(S2, f"psc{p}{c}", [128, 512]) for c in range(2)]
                       for p in range(3)]
                pav = [psm(S2, f"pav{i}", [128, 512]) for i in range(2)]

                with nc.Block() as blk:
                    @blk.gpsimd
                    def _(g):
                        g.memset(slop_s[:], 0.0)
                        g.memset(augr_s[:], 0.0)
                        gdma(g, slop_s[0:4, :], D["slopes4"])
                        gdma(g, augr_s[0:4, :], D["augrows"])
                        g.wait_ge(SG.sem, SG.n)

                    @blk.sync
                    def _(s):
                        sdma(s, bias_s[:], D["biascol"])
                        sdma(s, mask_s[:], D["masktiles"])
                        s.wait_ge(SS.sem, SS.n)

                groups = []
                for g_ in range(KVH):
                    for l in range(NL):
                        for hp in range(2):
                            nk = NKP[l]
                            ngr = (nk + 3) // 4
                            for gr in range(ngr):
                                groups.append(dict(
                                    g=g_, l=l, gr=gr, j0=gr * 4,
                                    nj=min(4, nk - gr * 4),
                                    heads=(g_ + hp * 8, g_ + hp * 8 + 4),
                                    first=(gr == 0), last=(gr == ngr - 1)))
                M = len(groups)
                seg_of, segend = {}, {}
                seg = -1
                for m, gr in enumerate(groups):
                    if gr["first"]:
                        seg += 1
                    seg_of[m] = seg
                    if gr["last"]:
                        segend[seg] = m
                NSEG = seg + 1
                SP0, SA0, SV0 = SP.n, SA.n, SV.n
                LAG = 2
                scores_done, sumsav_done = {}, {}
                pe_n = SP0
                for m in range(M):
                    pe_n += 1; scores_done[m] = pe_n
                    if m >= LAG:
                        pe_n += 1; sumsav_done[m - LAG] = pe_n
                for m in range(M - LAG, M):
                    pe_n += 1; sumsav_done[m] = pe_n
                exp_done = {m: SA0 + m + 1 for m in range(M)}
                mask_done, evac_done = {}, {}
                dve_n = SV0
                for s_ in range(NSEG):
                    dve_n += 1; mask_done[segend[s_]] = dve_n
                    dve_n += 2; evac_done[s_] = dve_n

                with nc.Block() as blk:
                    @blk.tensor
                    def _(t):
                        def sums_av(m):
                            gr = groups[m]
                            sg = seg_of[m]
                            t.wait_ge(SA.sem, exp_done[m])
                            if gr["first"] and sg > 1:
                                t.wait_ge(SV.sem, evac_done[sg - 2])
                            p = m % 3
                            pb = pav[sg % 2]
                            for ci in range(2):
                                for jl in range(gr["nj"]):
                                    j = gr["j0"] + jl
                                    ins = t.matmul(
                                        pb[:, ci * 128:(ci + 1) * 128],
                                        v_tok[j][:, gr["g"] * 128:(gr["g"] + 1) * 128],
                                        E[p][ci][:, jl * 128:(jl + 1) * 128],
                                        start=(gr["first"] and ci == 0 and jl == 0),
                                        stop=False,
                                        skip_group_check=True)
                            for ci in range(2):
                                for jl in range(gr["nj"]):
                                    ins = t.matmul(
                                        pb[:, 256 + ci * 128:256 + (ci + 1) * 128],
                                        onesf[:],
                                        E[p][ci][:, jl * 128:(jl + 1) * 128],
                                        start=False,
                                        stop=(gr["last"] and ci == 1 and jl == gr["nj"] - 1),
                                        skip_group_check=True)
                            SP.inc(ins)
                            SP.expect(sumsav_done[m])

                        def augs(m):
                            gr = groups[m]
                            p = m % 3
                            l = gr["l"]
                            for ci, h in enumerate(gr["heads"]):
                                j = gr["j0"]
                                t.matmul(
                                    psc[p][ci][:, 0:gr["nj"] * 128],
                                    slop_s[:, h * 128:(h + 1) * 128],
                                    augr_s[:, (l * 16 + j) * 128:
                                           (l * 16 + j + gr["nj"]) * 128],
                                    start=True, stop=False,
                                    skip_group_check=True)

                        for m, gr in enumerate(groups):
                            p = m % 3
                            l, g_ = gr["l"], gr["g"]
                            if m >= 3:
                                t.wait_ge(SA.sem, exp_done[m - 3])
                            augs(m)
                            for ci, h in enumerate(gr["heads"]):
                                for jl in range(gr["nj"]):
                                    j = gr["j0"] + jl
                                    ins = t.matmul(
                                        psc[p][ci][:, jl * 128:(jl + 1) * 128],
                                        k_fm[g_][:, j * 128:(j + 1) * 128],
                                        q_fm[h][:, l * 128:(l + 1) * 128],
                                        start=False, stop=(jl == gr["nj"] - 1),
                                        skip_group_check=True)
                            SP.inc(ins)
                            SP.expect(scores_done[m])
                            if m >= LAG:
                                sums_av(m - LAG)
                        for m in range(M - LAG, M):
                            sums_av(m)

                    @blk.vector
                    def _(v):
                        for s_ in range(NSEG):
                            m = segend[s_]
                            gr = groups[m]
                            p = m % 3
                            l = gr["l"]
                            o = (gr["nj"] - 2) * 128
                            v.wait_ge(SP.sem, scores_done[m])
                            for ci in range(2):
                                ins = v.tensor_tensor(
                                    psc[p][ci][:, o:o + 256],
                                    psc[p][ci][:, o:o + 256],
                                    mask_s[:, l * 256:(l + 1) * 256], ALU.add)
                            SV.inc(ins)
                            SV.expect(mask_done[m])
                            v.wait_ge(SP.sem, sumsav_done[m])
                            pb = pav[s_ % 2]
                            for ci in range(2):
                                ins = v.reciprocal(inv[ci][:],
                                                   pb[:, 256 + ci * 128:256 + (ci + 1) * 128])
                            vv = SV.inc(ins)
                            v.wait_ge(SV.sem, vv)
                            for ci, h in enumerate(gr["heads"]):
                                ins = v.tensor_tensor(
                                    oa_fm[h][:, l * 128:(l + 1) * 128],
                                    pb[:, ci * 128:(ci + 1) * 128],
                                    inv[ci][:], ALU.mult)
                            SV.inc(ins)
                            SV.expect(evac_done[s_])

                    @blk.scalar
                    def _(sc):
                        for m, gr in enumerate(groups):
                            p = m % 3
                            sc.wait_ge(SP.sem, scores_done[m])
                            if gr["last"]:
                                sc.wait_ge(SV.sem, mask_done[m])
                            for ci, h in enumerate(gr["heads"]):
                                ins = sc.activation(
                                    E[p][ci][:, 0:gr["nj"] * 128],
                                    psc[p][ci][:, 0:gr["nj"] * 128],
                                    AF.Exp, bias=bias_s[:, h:h + 1])
                            SA.inc(ins)
                            SA.expect(exp_done[m])

            if stage == 2:
                with nc.Block() as blk:
                    @blk.sync
                    def _(s):
                        for h in range(H):
                            sdma(s, dbg["attn"][h * 128:(h + 1) * 128, :], oa_fm[h][:])
                        s.wait_ge(SS.sem, SS.n)
                return nc, D

            SCB.close()
            S34 = ExitStack()
            wgb = [sb(S34, f"wgb{i}", [128, C], BF16, side="right") for i in range(3)]
            wub = [sb(S34, f"wub{i}", [128, C], BF16, side="right") for i in range(3)]
            xq_fm = [sb(S34, f"xqf{c}", [128, TL], BF16, side="right")
                     for c in range(16)]
            NPRE = 0 if os.environ.get("DEVK_NOPRE") else 3
            with ExitStack() as S3:
                x1p = [sb(S3, f"x1p{l}", [128, C], F32) for l in range(NL)]
                wob = [sb(S3, f"wob{i}", [128, 8 * 512], BF16, side="right")
                       for i in range(3)]
                x1n = sb(S3, "x1n", [128, C], F32)
                sqs = x1n
                xqb = [sb(S3, f"xqb{i}", [128, C], BF16) for i in range(2)]
                cn = ("sum", "sq", "mean", "var", "msq", "rstd", "nb", "amax",
                      "iam", "s127", "a127", "lnv")
                col = {nm: sb(S3, "c_" + nm, [128, 1], F32) for nm in cn}
                py = [psm(S3, f"py{i}", [128, 512]) for i in range(8)]

                SP0, SA0, SV0, SS0, SG0 = SP.n, SA.n, SV.n, SS.n, SG.n
                ymm = lambda m: SP0 + m + 1
                wo_ld = lambda kh: SG0 + 16 * (kh + 1)
                xres_ld = lambda l: SS0 + 16 * (l + 1)
                x1_st = lambda l: SS0 + 128 + 32 * l + 16
                xq_st = lambda l: SS0 + 128 + 32 * l + 32
                _yevM, _VCH = {}, {}
                _n = SV0
                for _m in range(32):
                    _n += 1
                    _yevM[_m] = _n
                    _k, _lh = divmod(_m, 4)
                    _hf, _fk = divmod(_k, 4)
                    if _fk == 3:
                        _l = _hf * 4 + _lh
                        for _i in range(1, 9):
                            _VCH[(_l, _i)] = _n + _i
                        _n += 8
                yevM = lambda m: _yevM[m]
                VCH = lambda l, i: _VCH[(l, i)]
                ACH = lambda l, i: SA0 + 6 * l + i
                SGW = SG0 + 256
                wldF = lambda hc, mat: SGW + 16 * (hc * 2 + mat + 1)
                XQ7 = SS0 + 128 + 32 * 7 + 32
                XSP7 = SS0 + 128 + 32 * 7 + 16
                xqfF = lambda c, _b=XQ7: _b + 16 * (c + 1)

                with nc.Block() as blk:
                    @blk.gpsimd
                    def _(g):
                        for kh in range(16):
                            k, half = divmod(kh, 2)
                            fk = k % 4
                            if kh >= 3:
                                g.wait_ge(SP.sem, ymm(((kh - 3) // 2) * 4 + 3))
                            gdma(g, wob[kh % 3][:],
                                 D["wo"][fk * 128:(fk + 1) * 128,
                                         half * 4096:(half + 1) * 4096])
                            assert SG.n == wo_ld(kh)
                        for hc in range(NPRE):
                            for mat, dr in ((0, "wgt"), (1, "wut")):
                                gdma(g, (wgb if mat == 0 else wub)[hc % 3][:],
                                     D[dr][hc * 128:(hc + 1) * 128, :])
                                assert SG.n == wldF(hc, mat)

                    @blk.sync
                    def _(s):
                        for l in range(NL):
                            sdma(s, x1p[l][:], D["xres"][l * 128:(l + 1) * 128, :])
                            assert SS.n == xres_ld(l)
                        for l in range(NL):
                            s.wait_ge(SV.sem, VCH(l, 3))
                            sdma(s, D["x1d"][l * 128:(l + 1) * 128, :], x1p[l][:])
                            s.wait_ge(SV.sem, VCH(l, 8))
                            sdma(s, D["xqd"][l * 128:(l + 1) * 128, :], xqb[l % 2][:])
                            assert SS.n == xq_st(l)


                    @blk.tensor
                    def _(t):
                        for k in range(8):
                            hf, fk = divmod(k, 4)
                            t.wait_ge(SG.sem, wo_ld(k * 2))
                            for lh in range(4):
                                m = k * 4 + lh
                                l = hf * 4 + lh
                                if m >= 8:
                                    t.wait_ge(SV.sem, yevM(m - 8))
                                if lh == 0:
                                    t.wait_ge(SG.sem, wo_ld(k * 2 + 1))
                                t.matmul(py[m % 8][:], ones2p[:],
                                         bo2_s[:, fk * 512:(fk + 1) * 512],
                                         start=True, stop=False,
                                         skip_group_check=True)
                                for h in range(H):
                                    ins = t.matmul(
                                        py[m % 8][:],
                                        oa_fm[h][:, l * 128:(l + 1) * 128],
                                        wob[(k * 2 + h // 8) % 3]
                                        [:, (h % 8) * 512:(h % 8 + 1) * 512],
                                        start=False, stop=(h == H - 1),
                                        skip_group_check=True)
                                SP.inc(ins)
                                SP.expect(ymm(m))

                    @blk.vector
                    def _(v):
                        def evac(m):
                            k, lh = divmod(m, 4)
                            hf, fk = divmod(k, 4)
                            l = hf * 4 + lh
                            v.wait_ge(SP.sem, ymm(m))
                            if fk == 0:
                                v.wait_ge(SS.sem, xres_ld(l))
                            ins = v.tensor_tensor(
                                x1p[l][:, fk * 512:(fk + 1) * 512],
                                x1p[l][:, fk * 512:(fk + 1) * 512],
                                py[m % 8][:], ALU.add)
                            SV.inc(ins)
                            SV.expect(yevM(m))
                        for m in range(32):
                            evac(m)
                            k, lh = divmod(m, 4)
                            hf, fk = divmod(k, 4)
                            if fk != 3:
                                continue
                            l = hf * 4 + lh
                            v.wait_ge(SA.sem, ACH(l, 1))
                            ins = v.reduce_sum(col["sum"][:], x1p[l][:], AX.X)
                            vv = SV.inc(ins)
                            v.wait_ge(SV.sem, vv)
                            v.tensor_scalar_mul(col["mean"][:], col["sum"][:], 1.0 / C)
                            v.tensor_scalar_mul(col["var"][:], col["sq"][:], 1.0 / C)
                            v.tensor_tensor(col["msq"][:], col["mean"][:],
                                            col["mean"][:], ALU.mult)
                            v.tensor_tensor(col["var"][:], col["var"][:],
                                            col["msq"][:], ALU.subtract)
                            ins = v.tensor_scalar_add(col["var"][:], col["var"][:], EPS_LN)
                            SV.inc(ins)
                            v.wait_ge(SA.sem, ACH(l, 3))
                            v.scalar_tensor_tensor(col["nb"][:], col["mean"][:], -1.0,
                                                   col["rstd"][:], ALU.mult, ALU.mult)
                            ins = v.tensor_scalar(x1p[l][:], x1p[l][:], col["rstd"][:],
                                                  col["nb"][:], ALU.mult, ALU.add)
                            SV.inc(ins)
                            v.wait_ge(SA.sem, ACH(l, 4))
                            ins = v.reduce_sum(col["sum"][:], x1p[l][:], AX.X)
                            vv = SV.inc(ins)
                            v.wait_ge(SV.sem, vv)
                            v.tensor_scalar_mul(col["mean"][:], col["sum"][:], 1.0 / C)
                            v.tensor_scalar_mul(col["var"][:], col["sq"][:], 1.0 / C)
                            v.tensor_tensor(col["msq"][:], col["mean"][:],
                                            col["mean"][:], ALU.mult)
                            v.tensor_tensor(col["var"][:], col["var"][:],
                                            col["msq"][:], ALU.subtract)
                            ins = v.tensor_scalar_add(col["var"][:], col["var"][:], EPS_RMS)
                            SV.inc(ins)
                            v.wait_ge(SA.sem, ACH(l, 6))
                            v.scalar_tensor_tensor(col["nb"][:], col["mean"][:], -1.0,
                                                   col["rstd"][:], ALU.mult, ALU.mult)
                            v.tensor_scalar(x1n[:], x1p[l][:], col["rstd"][:],
                                            col["nb"][:], ALU.mult, ALU.add)
                            ins = v.tensor_reduce(col["amax"][:], x1n[:], AX.X,
                                                  ALU.max, apply_absolute_value=True)
                            vv = SV.inc(ins)
                            v.wait_ge(SV.sem, vv)
                            ins = v.reciprocal(col["iam"][:], col["amax"][:])
                            vv = SV.inc(ins)
                            v.wait_ge(SV.sem, vv)
                            v.tensor_scalar_mul(col["s127"][:], col["iam"][:], 127.0)
                            v.tensor_scalar_mul(col["a127"][:], col["amax"][:], 1.0 / 127.0)
                            if l >= 2:
                                v.wait_ge(SS.sem, xq_st(l - 2))
                            v.tensor_scalar(x1n[:], x1n[:], col["s127"][:], MAGIC,
                                            ALU.mult, ALU.add)
                            v.tensor_scalar_sub(x1n[:], x1n[:], MAGIC)
                            ins = v.tensor_scalar_mul(xqb[l % 2][:], x1n[:], col["a127"][:])
                            SV.inc(ins)
                            SV.expect(VCH(l, 8))

                    @blk.scalar
                    def _(sc):
                        for l in range(NL):
                            sc.wait_ge(SV.sem, yevM(((l // 4) * 4 + 3) * 4 + l % 4))
                            ins = sc.activation(sqs[:], x1p[l][:], AF.Square,
                                                accum_out=col["sq"][:])
                            SA.inc(ins)
                            sc.wait_ge(SV.sem, VCH(l, 2))
                            ins = sc.activation(col["lnv"][:], col["var"][:], AF.Ln)
                            aa = SA.inc(ins)
                            sc.wait_ge(SA.sem, aa)
                            ins = sc.activation(col["rstd"][:], col["lnv"][:],
                                                AF.Exp, scale=-0.5)
                            SA.inc(ins)
                            sc.wait_ge(SV.sem, VCH(l, 3))
                            ins = sc.activation(sqs[:], x1p[l][:], AF.Square,
                                                accum_out=col["sq"][:])
                            SA.inc(ins)
                            sc.wait_ge(SV.sem, VCH(l, 5))
                            ins = sc.activation(col["lnv"][:], col["var"][:], AF.Ln)
                            aa = SA.inc(ins)
                            sc.wait_ge(SA.sem, aa)
                            ins = sc.activation(col["rstd"][:], col["lnv"][:],
                                                AF.Exp, scale=-0.5)
                            SA.inc(ins)
                            SA.expect(ACH(l, 6))

            if stage == 3:
                with nc.Block() as blk:
                    @blk.sync
                    def _(s):
                        s.wait_ge(SS.sem, SS.n)
                return nc, D

        with ExitStack() as S4:
            p_fm = [sb(S4, f"pfm{hc}", [128, TL], BF16) for hc in range(64)]
            S4a = S4.enter_context(ExitStack())
            gsg = [sb(S4a, f"gsg{i}", [128, 512], BF16) for i in range(4)]
            pgu = [psm(S4a, f"pgu{i}", [128, 512]) for i in range(8)]
            SP0, SA0, SV0, SS0, SG0 = SP.n, SA.n, SV.n, SS.n, SG.n
            wld = wldF
            sgn_done = lambda hc, mat: SA0 + hc * 4 + mat + 1
            silu_done = lambda hc, th: SA0 + hc * 4 + 2 + th + 1
            gu = lambda hc, mat, th: SP0 + hc * 4 + mat * 2 + th + 1
            pd = lambda hc, th: SV0 + hc * 2 + th + 1

            with nc.Block() as blk:
                @blk.sync
                def _(s):
                    s.wait_ge(SG.sem, wldF(NPRE - 1, 1))
                    s.wait_ge(SS.sem, xqfF(0) - 16)
                    for c in range(16):
                        sdma(s, xq_fm[c][:],
                             D["xqd"][:, c * 128:(c + 1) * 128],
                             transpose=True)
                        assert SS.n == xqfF(c)

                @blk.gpsimd
                def _(g):
                    g.wait_ge(SS.sem, xqfF(15))
                    for hc in range(NPRE, 64):
                        for mat, dr in ((0, "wgt"), (1, "wut")):
                            g.wait_ge(SP.sem, gu(hc - 3, mat, 1))
                            gdma(g, (wgb if mat == 0 else wub)[hc % 3][:],
                                 D[dr][hc * 128:(hc + 1) * 128, :])
                            assert SG.n == wld(hc, mat)

                @blk.scalar
                def _(sc):
                    for hc in range(64):
                        for mat in range(2):
                            wb = (wgb if mat == 0 else wub)[hc % 3]
                            sc.wait_ge(SG.sem, wld(hc, mat))
                            if hc >= 3:
                                sc.wait_ge(SP.sem, gu(hc - 3, mat, 1))
                            ins = sc.activation(wb[:], wb[:], AF.Sign)
                            SA.inc(ins)
                            SA.expect(sgn_done(hc, mat))
                        for th in range(2):
                            sc.wait_ge(SP.sem, gu(hc, 0, th))
                            if hc >= 2:
                                sc.wait_ge(SV.sem, pd(hc - 2, th))
                            bank = 0 * 4 + th * 2 + hc % 2
                            ins = sc.activation(gsg[th * 2 + hc % 2][:],
                                                pgu[bank][:], AF.Silu)
                            SA.inc(ins)
                            SA.expect(silu_done(hc, th))

                @blk.tensor
                def _(t):
                    t.wait_ge(SS.sem, xqfF(15))
                    for hc in range(64):
                        for mat in range(2):
                            wb = (wgb if mat == 0 else wub)[hc % 3]
                            t.wait_ge(SA.sem, sgn_done(hc, mat))
                            for th in range(2):
                                bank = mat * 4 + th * 2 + hc % 2
                                if hc >= 2:
                                    t.wait_ge(SV.sem, pd(hc - 2, th))
                                for c in range(16):
                                    ins = t.matmul(
                                        pgu[bank][:],
                                        wb[:, c * 128:(c + 1) * 128],
                                        xq_fm[c][:, th * 512:(th + 1) * 512],
                                        start=(c == 0), stop=(c == 15))
                                SP.inc(ins)
                                SP.expect(gu(hc, mat, th))

                @blk.vector
                def _(v):
                    for hc in range(64):
                        for th in range(2):
                            v.wait_ge(SA.sem, silu_done(hc, th))
                            v.wait_ge(SP.sem, gu(hc, 1, th))
                            ubank = 1 * 4 + th * 2 + hc % 2
                            ins = v.tensor_tensor(
                                p_fm[hc][:, th * 512:(th + 1) * 512],
                                gsg[th * 2 + hc % 2][:], pgu[ubank][:], ALU.mult)
                            SV.inc(ins)
                            SV.expect(pd(hc, th))

            if stage == 34:
                with nc.Block() as blk:
                    @blk.sync
                    def _(s):
                        for hc in range(64):
                            sdma(s, dbg["p"][hc * 128:(hc + 1) * 128, :], p_fm[hc][:])
                        s.wait_ge(SS.sem, SS.n)
                return nc, D

            S4a.close()
            S34.close()
            with ExitStack() as S5:
                wdb = [sb(S5, f"wdb{i}", [128, 64 * 128], BF16) for i in range(2)]
                x1bf = [sb(S5, f"x1bf{i}", [128, C], BF16) for i in range(NL)]
                col6 = {nm: sb(S5, "d_" + nm, [128, 1], F32) for nm in
                        ("sum", "sq", "mean", "var", "msq", "rstd", "nb", "lnv")}
                pdn = [psm(S5, f"pdn{i}", [128, 512]) for i in range(4)]
                SP0, SA0, SV0, SS0, SG0 = SP.n, SA.n, SV.n, SS.n, SG.n
                wdld = lambda sx: (SG0 + 16 * (sx + 1) if sx < 2
                                   else SG0 + 160 + 16 * (sx - 1))
                wsgn = lambda sx: SA0 + sx + 1
                dn = lambda m: SP0 + m + 1
                x1bf_ld = lambda l: SG0 + 32 + 16 * (l + 1)
                y_st = lambda l: SG0 + 384 + 16 * (l + 1)
                _yev5, _V6 = {}, {}
                _n = SV0
                for _m in range(128):
                    _n += 1
                    _yev5[_m] = _n
                    if _m >= 120:
                        _l = _m - 120
                        for _i in range(1, 4):
                            _V6[(_l, _i)] = _n + _i
                        _n += 3
                yev5 = lambda m: _yev5[m]
                V6_ = lambda l, i: _V6[(l, i)]
                A6_ = lambda l, i: SA0 + 16 + 3 * l + i

                with nc.Block() as blk:
                    @blk.gpsimd
                    def _(g):
                        for sx in range(2):
                            gdma(g, wdb[sx % 2][:],
                                 D["wdt"][sx * 128:(sx + 1) * 128, :])
                            assert SG.n == wdld(sx)
                        g.wait_ge(SS.sem, XSP7)
                        for l in range(NL):
                            gdma(g, x1bf[l][:], D["x1d"][l * 128:(l + 1) * 128, :])
                            assert SG.n == x1bf_ld(l)
                        for sx in range(2, 16):
                            g.wait_ge(SP.sem, dn((sx - 2) * 8 + 7))
                            gdma(g, wdb[sx % 2][:],
                                 D["wdt"][sx * 128:(sx + 1) * 128, :])
                            assert SG.n == wdld(sx)
                        for l in range(NL):
                            g.wait_ge(SV.sem, V6_(l, 3))
                            gdma(g, D["y"][l * 128:(l + 1) * 128, :], x1bf[l][:])
                            assert SG.n == y_st(l)
                        g.wait_ge(SG.sem, SG.n)

                    @blk.scalar
                    def _(sc):
                        for sx in range(16):
                            sc.wait_ge(SG.sem, wdld(sx))
                            if sx >= 2:
                                sc.wait_ge(SP.sem, dn((sx - 2) * 8 + 7))
                            ins = sc.activation(wdb[sx % 2][:], wdb[sx % 2][:], AF.Sign)
                            SA.inc(ins)
                            SA.expect(wsgn(sx))
                        for l in range(NL):
                            sc.wait_ge(SV.sem, V6_(l, 1))
                            ins = sc.activation(wdb[0][:, 0:C], x1bf[l][:], AF.Square,
                                                accum_out=col6["sq"][:])
                            SA.inc(ins)
                            sc.wait_ge(SV.sem, V6_(l, 2))
                            ins = sc.activation(col6["lnv"][:], col6["var"][:], AF.Ln)
                            aa = SA.inc(ins)
                            sc.wait_ge(SA.sem, aa)
                            ins = sc.activation(col6["rstd"][:], col6["lnv"][:],
                                                AF.Exp, scale=-0.5)
                            SA.inc(ins)
                            SA.expect(A6_(l, 3))

                    @blk.tensor
                    def _(t):
                        for sx in range(16):
                            t.wait_ge(SA.sem, wsgn(sx))
                            for l in range(NL):
                                m = sx * 8 + l
                                if m >= 4:
                                    t.wait_ge(SV.sem, yev5(m - 4))
                                for hc in range(64):
                                    ins = t.matmul(
                                        pdn[m % 4][:, 0:128],
                                        p_fm[hc][:, l * 128:(l + 1) * 128],
                                        wdb[sx % 2][:, hc * 128:(hc + 1) * 128],
                                        start=(hc == 0), stop=(hc == 63))
                                SP.inc(ins)
                                SP.expect(dn(m))

                    @blk.vector
                    def _(v):
                        for m in range(128):
                            sx_, l_ = divmod(m, 8)
                            v.wait_ge(SP.sem, dn(m))
                            if m == 0:
                                v.wait_ge(SG.sem, x1bf_ld(7))
                            ins = v.scalar_tensor_tensor(
                                x1bf[l_][:, sx_ * 128:(sx_ + 1) * 128],
                                pdn[m % 4][:, 0:128], 1.0,
                                x1bf[l_][:, sx_ * 128:(sx_ + 1) * 128],
                                ALU.mult, ALU.add)
                            SV.inc(ins)
                            SV.expect(yev5(m))
                            if m < 120:
                                continue
                            l = m - 120
                            ins = v.reduce_sum(col6["sum"][:], x1bf[l][:], AX.X)
                            vv = SV.inc(ins)
                            v.wait_ge(SV.sem, vv)
                            v.wait_ge(SA.sem, A6_(l, 1))
                            v.tensor_scalar_mul(col6["mean"][:], col6["sum"][:], 1.0 / C)
                            v.tensor_scalar_mul(col6["var"][:], col6["sq"][:], 1.0 / C)
                            v.tensor_tensor(col6["msq"][:], col6["mean"][:],
                                            col6["mean"][:], ALU.mult)
                            v.tensor_tensor(col6["var"][:], col6["var"][:],
                                            col6["msq"][:], ALU.subtract)
                            ins = v.tensor_scalar_add(col6["var"][:], col6["var"][:],
                                                      EPS_LN)
                            SV.inc(ins)
                            v.wait_ge(SA.sem, A6_(l, 3))
                            v.scalar_tensor_tensor(col6["nb"][:], col6["mean"][:], -1.0,
                                                   col6["rstd"][:], ALU.mult, ALU.mult)
                            ins = v.tensor_scalar(x1bf[l][:], x1bf[l][:],
                                                  col6["rstd"][:], col6["nb"][:],
                                                  ALU.mult, ALU.add)
                            SV.inc(ins)
                            SV.expect(V6_(l, 3))

    return nc, D


def kernel(**inputs):
    nc, _ = build(stage=5)
    in_maps = [stage_core(inputs, c) for c in range(8)]
    res = run_bass_kernel_spmd(nc, in_maps, core_ids=list(range(8)))
    out = np.empty((B, T, C), dtype=np.float32)
    for c in range(8):
        b, par = c // 2, c % 2
        y = np.asarray(res.results[c]["y"])
        for li, g in enumerate(TILES[par]):
            out[b, g * 128:(g + 1) * 128, :] = y[li * 128:(li + 1) * 128, :]
    return out


# version 20 (final)
# speedup vs baseline: 1.1547x; 1.0444x over previous
import os
import numpy as np
import ml_dtypes
import concourse.bass as bass
import concourse.mybir as mybir
from concourse.bass_utils import run_bass_kernel_spmd
from contextlib import ExitStack

F32 = mybir.dt.float32
BF16 = mybir.dt.bfloat16
AF = mybir.ActivationFunctionType
ALU = mybir.AluOpType
AX = mybir.AxisListType
bfd = ml_dtypes.bfloat16

B, T, C = 4, 2048, 2048
H, KVH, HD = 16, 4, 128
NT, NL, TL = 16, 8, 1024
HF = 8192
SC = 1.0 / np.sqrt(128.0)
MAGIC = 1.5 * 2.0 ** 23
EPS_LN = 1e-5
EPS_RMS = 1e-8
NEG = -1.0e30
TILES = {0: [0, 3, 4, 7, 8, 11, 12, 15], 1: [1, 2, 5, 6, 9, 10, 13, 14]}
NKP = [2 * (l + 1) for l in range(NL)]


def _hilo(a):
    hi = a.astype(bfd).astype(np.float32)
    lo = (a - hi).astype(bfd).astype(np.float32)
    return hi, lo


def _slopes():
    start = 2.0 ** (-(2.0 ** -(np.log2(H) - 3.0)))
    return (start ** (np.arange(H) + 1)).astype(np.float64)


def _tile_w(w, fsz):
    K, F = w.shape
    a = w.reshape(K // 128, 128, F // fsz, fsz)
    return np.ascontiguousarray(a.transpose(2, 1, 0, 3)).reshape(
        F // fsz * 128, K // 128 * fsz)


def stage_core(inp, core):
    b, par = core // 2, core % 2
    tiles = TILES[par]
    x = np.asarray(inp["L0_attn_x"][b], dtype=np.float32)
    own = np.concatenate([np.arange(g * 128, (g + 1) * 128) for g in tiles])
    sl = _slopes()
    s_hi, s_lo = _hilo(sl.astype(np.float32))
    slopes4 = np.zeros((4, H * 128), dtype=np.float32)
    for h in range(H):
        slopes4[0, h * 128:(h + 1) * 128] = s_hi[h]
        slopes4[1, h * 128:(h + 1) * 128] = s_hi[h]
        slopes4[2, h * 128:(h + 1) * 128] = s_lo[h]
        slopes4[3, h * 128:(h + 1) * 128] = s_lo[h]
    biascol = (np.asarray(sl)[None, :] * np.arange(128)[:, None]).astype(np.float32)
    augrows = np.zeros((4, NL * 16 * 128), dtype=np.float32)
    for l in range(NL):
        G = tiles[l]
        for j in range(NKP[l]):
            off = (l * 16 + j) * 128
            if j <= G:
                r = (128.0 * (j - G) - np.arange(128)).astype(np.float32)
                hi, lo = _hilo(r)
            else:
                hi = np.full(128, NEG, dtype=np.float32)
                lo = np.full(128, NEG, dtype=np.float32)
            augrows[0, off:off + 128] = hi
            augrows[1, off:off + 128] = lo
            augrows[2, off:off + 128] = hi
            augrows[3, off:off + 128] = lo
    masktiles = np.zeros((128, NL * 256), dtype=np.float32)
    tri = np.where(np.arange(128)[:, None] > np.arange(128)[None, :], NEG, 0.0
                   ).astype(np.float32)
    for l in range(NL):
        G, nk = tiles[l], NKP[l]
        for slot, j in enumerate((nk - 2, nk - 1)):
            if j == G:
                masktiles[:, l * 256 + slot * 128: l * 256 + (slot + 1) * 128] = tri
    bo_hi, bo_lo = _hilo(np.asarray(inp["L0_attn_bo"], dtype=np.float32))
    return {
        "xT": np.ascontiguousarray(x.T),
        "xTq": np.ascontiguousarray(x.T[:, own]),
        "xres": np.ascontiguousarray(x[own]),
        "wq": _tile_w(np.asarray(inp["L0_attn_Wq"], np.float32), 128),
        "wkvk": _tile_w(np.asarray(inp["L0_attn_Wkv"], np.float32)[:, :512], 512),
        "wkvv": _tile_w(np.asarray(inp["L0_attn_Wkv"], np.float32)[:, 512:], 512),
        "wo": _tile_w(np.asarray(inp["L0_attn_Wo"], np.float32), 512),
        "bo2": np.stack([bo_hi, bo_lo]),
        "slopes4": slopes4, "biascol": biascol,
        "augrows": augrows, "masktiles": masktiles,
        "wgt": _tile_w(np.asarray(inp["L0_ffn_W_g"], np.float32).T, 128),
        "wut": _tile_w(np.asarray(inp["L0_ffn_W_u"], np.float32).T, 128),
        "wdt": _tile_w(np.asarray(inp["L0_ffn_W_d"], np.float32).T, 128),
    }


def build(stage=5):
    nc = bass.Bass("TRN2", target_bir_lowering=False)
    D = {}
    for nm, shape, dt in (
            ("xT", [C, T], F32), ("xTq", [C, TL], F32), ("xres", [TL, C], F32),
            ("wq", [C, C], F32), ("wkvk", [128, 16 * 512], F32),
            ("wkvv", [128, 16 * 512], F32), ("wo", [512, 16 * 512], F32),
            ("bo2", [2, C], F32), ("slopes4", [4, H * 128], F32),
            ("biascol", [128, 16], F32), ("augrows", [4, NL * 16 * 128], F32),
            ("masktiles", [128, NL * 256], F32),
            ("wgt", [HF, C], F32), ("wut", [HF, C], F32),
            ("wdt", [C, HF], F32)):
        D[nm] = nc.dram_tensor(nm, shape, dt, kind="ExternalInput").ap()
    D["y"] = nc.dram_tensor("y", [TL, C], F32, kind="ExternalOutput").ap()
    D["wob16"] = nc.dram_tensor("wob16", [512, 16 * 512], BF16, kind="Internal").ap()
    sk3 = "ExternalOutput" if stage in (3, 4) else "Internal"
    sk4 = "ExternalOutput" if stage in (4, 45) else "Internal"
    D["x1d"] = nc.dram_tensor("x1d", [TL, C], F32, kind=sk3).ap()
    D["xqd"] = nc.dram_tensor("xqd", [TL, C], BF16, kind=sk3).ap()
    D["y2d"] = nc.dram_tensor("y2d", [TL, C], F32, kind=sk4).ap()
    dbg = {}
    if stage == 1:
        dbg["q"] = nc.dram_tensor("dbg_q", [H * 128, TL], BF16, kind="ExternalOutput").ap()
        dbg["k"] = nc.dram_tensor("dbg_k", [KVH * 128, T], BF16, kind="ExternalOutput").ap()
        dbg["v"] = nc.dram_tensor("dbg_v", [T, 512], BF16, kind="ExternalOutput").ap()
    if stage == 2:
        dbg["attn"] = nc.dram_tensor("dbg_attn", [H * 128, TL], BF16, kind="ExternalOutput").ap()
    if stage == 34:
        dbg["p"] = nc.dram_tensor("dbg_p", [64 * 128, TL], BF16, kind="ExternalOutput").ap()

    class Ct:
        def __init__(self, name):
            self.sem = nc.semaphore(name).__enter__()
            self.n = 0
        def inc(self, ins, k=1):
            ins.then_inc(self.sem, k)
            self.n += k
            return self.n
        def expect(self, v):
            assert self.n == v, (self.n, v)

    SG, SS, SP, SA, SV = Ct("SG"), Ct("SS"), Ct("SP"), Ct("SA"), Ct("SV")

    def gdma(g, dst, src):
        return SG.inc(g.dma_start(dst, src), 16)

    def sdma(s, dst, src, **kw):
        return SS.inc(s.dma_start(dst, src, **kw), 16)

    def gdma3(g, dst, src, inner):
        return SG.inc(g.dma_start(
            dst[:].rearrange("p (c f) -> p c f", f=inner),
            src.rearrange("(c p) f -> p c f", p=128)), 16)

    with ExitStack() as SCO:
        def sb(st, name, shape, dt, side=None):
            return st.enter_context(nc.sbuf_tensor(name, shape, dt, side=side))
        def psm(st, name, shape):
            return st.enter_context(nc.psum_tensor(name, shape, F32))
        ones2p = sb(SCO, "ones2p", [128, 128], BF16)
        onesf = sb(SCO, "onesf", [128, 128], BF16)

        with ExitStack() as SCA:
            bo2_s = sb(SCA, "bo2_s", [128, C], BF16)
            oa_fm = [sb(SCA, f"oafm{h}", [128, TL], BF16) for h in range(H)]
            SCB = SCA.enter_context(ExitStack())
            q_fm = [sb(SCB, f"qfm{h}", [128, TL], BF16) for h in range(H)]
            k_fm = [sb(SCB, f"kfm{g}", [128, T], BF16) for g in range(KVH)]
            v_tok = [sb(SCB, f"vtok{j}", [128, 512], BF16) for j in range(NT)]

            with ExitStack() as S1:
                x_fm = [sb(S1, f"xfm{d}", [128, T], BF16) for d in range(16)]
                with ExitStack() as S1a:
                    xfq = [sb(S1a, f"xfq{d}", [128, TL], BF16) for d in range(16)]
                    wqb = [sb(S1a, f"wqb{i}", [128, C], BF16) for i in range(2)]
                    pq = [psm(S1a, f"pq{i}", [128, 512]) for i in range(2)]

                    SP0, SA0, SG0 = SP.n, SA.n, SG.n
                    qmm = lambda m: SP0 + m + 1
                    qev = lambda m: SA0 + m + 1
                    bo2_ld = SG0 + 16
                    xfq_ld = lambda d: SG0 + 16 + 16 * (d + 1)
                    ld_seq = []
                    for h in range(2):
                        ld_seq.append(("wq", h))
                    for h in range(2, H):
                        ld_seq.append(("wq", h))
                        ld_seq.append(("xf", h - 2))
                    for d in range(14, 16):
                        ld_seq.append(("xf", d))
                    base = SG0 + 16 + 256
                    wq_ld, xf_ld = {}, {}
                    for i, (kind, idx) in enumerate(ld_seq):
                        (wq_ld if kind == "wq" else xf_ld)[idx] = base + 16 * (i + 1)

                    with nc.Block() as blk:
                        @blk.gpsimd
                        def _(g):
                            g.memset(ones2p[:], 0.0)
                            g.memset(ones2p[0:2, :], 1.0)
                            g.memset(onesf[:], 1.0)
                            g.memset(bo2_s[:], 0.0)
                            gdma(g, bo2_s[0:2, :], D["bo2"])
                            for d in range(16):
                                gdma(g, xfq[d][:], D["xTq"][d * 128:(d + 1) * 128, :])
                            for kind, idx in ld_seq:
                                if kind == "wq":
                                    h = idx
                                    if h >= 2:
                                        g.wait_ge(SP.sem, qmm((h - 2) * 2 + 1))
                                    gdma(g, wqb[h % 2][:],
                                         D["wq"][h * 128:(h + 1) * 128, :])
                                else:
                                    gdma(g, x_fm[idx][:],
                                         D["xT"][idx * 128:(idx + 1) * 128, :])
                            SG.expect(base + 16 * len(ld_seq))

                        @blk.tensor
                        def _(t):
                            t.wait_ge(SG.sem, xfq_ld(15))
                            for h in range(H):
                                t.wait_ge(SG.sem, wq_ld[h])
                                for tc in range(2):
                                    m = h * 2 + tc
                                    if m >= 2:
                                        t.wait_ge(SA.sem, qev(m - 2))
                                    for c in range(16):
                                        ins = t.matmul(
                                            pq[m % 2][:],
                                            wqb[h % 2][:, c * 128:(c + 1) * 128],
                                            xfq[c][:, tc * 512:(tc + 1) * 512],
                                            start=(c == 0), stop=(c == 15))
                                    SP.inc(ins)
                                    SP.expect(qmm(m))

                        @blk.scalar
                        def _(sc):
                            for m in range(32):
                                h, tc = m // 2, m % 2
                                sc.wait_ge(SP.sem, qmm(m))
                                ins = sc.mul(q_fm[h][:, tc * 512:(tc + 1) * 512],
                                             pq[m % 2][:], SC)
                                SA.inc(ins)
                                SA.expect(qev(m))

                with ExitStack() as S1b:
                    wkvk = sb(S1b, "wkvk_s", [128, 16 * 512], BF16)
                    wkvv = sb(S1b, "wkvv_s", [128, 16 * 512], BF16)
                    pk = [psm(S1b, f"pk{i}", [128, 512]) for i in range(2)]
                    SP0, SA0, SG0 = SP.n, SA.n, SG.n
                    kmm = lambda m: SP0 + m + 1
                    kev = lambda m: SA0 + m + 1
                    with nc.Block() as blk:
                        @blk.gpsimd
                        def _(g):
                            gdma(g, wkvk[:], D["wkvk"])
                            gdma(g, wkvv[:], D["wkvv"])

                        @blk.tensor
                        def _(t):
                            t.wait_ge(SG.sem, SG.n)
                            m = 0
                            for g_ in range(KVH):
                                for tc in range(4):
                                    if m >= 2:
                                        t.wait_ge(SA.sem, kev(m - 2))
                                    for c in range(16):
                                        ins = t.matmul(
                                            pk[m % 2][:],
                                            wkvk[:, c * 512 + g_ * 128:
                                                 c * 512 + (g_ + 1) * 128],
                                            x_fm[c][:, tc * 512:(tc + 1) * 512],
                                            start=(c == 0), stop=(c == 15))
                                    SP.inc(ins); m += 1
                            for j in range(NT):
                                if m >= 2:
                                    t.wait_ge(SA.sem, kev(m - 2))
                                for c in range(16):
                                    ins = t.matmul(
                                        pk[m % 2][:],
                                        x_fm[c][:, j * 128:(j + 1) * 128],
                                        wkvv[:, c * 512:(c + 1) * 512],
                                        start=(c == 0), stop=(c == 15))
                                SP.inc(ins); m += 1
                            SP.expect(kmm(31))

                        @blk.scalar
                        def _(sc):
                            m = 0
                            for g_ in range(KVH):
                                for tc in range(4):
                                    sc.wait_ge(SP.sem, kmm(m))
                                    ins = sc.copy(k_fm[g_][:, tc * 512:(tc + 1) * 512],
                                                  pk[m % 2][:])
                                    SA.inc(ins); m += 1
                            for j in range(NT):
                                sc.wait_ge(SP.sem, kmm(m))
                                ins = sc.copy(v_tok[j][:], pk[m % 2][:])
                                SA.inc(ins); m += 1
                            SA.expect(kev(31))

            if stage == 1:
                with nc.Block() as blk:
                    @blk.sync
                    def _(s):
                        for h in range(H):
                            sdma(s, dbg["q"][h * 128:(h + 1) * 128, :], q_fm[h][:])
                        for g_ in range(KVH):
                            sdma(s, dbg["k"][g_ * 128:(g_ + 1) * 128, :], k_fm[g_][:])
                        for j in range(NT):
                            sdma(s, dbg["v"][j * 128:(j + 1) * 128, :], v_tok[j][:])
                        s.wait_ge(SS.sem, SS.n)
                return nc, D

            with ExitStack() as S2:
                slop_s = sb(S2, "slop", [128, H * 128], BF16)
                augr_s = sb(S2, "augr", [128, NL * 16 * 128], BF16)
                bias_s = sb(S2, "biasc", [128, 16], F32)
                mask_s = sb(S2, "maskt", [128, NL * 256], F32)
                E = [[sb(S2, f"E{p}{c}", [128, 512], BF16) for c in range(2)]
                     for p in range(3)]
                inv = [sb(S2, f"inv{c}", [128, 128], F32) for c in range(2)]
                psc = [[psm(S2, f"psc{p}{c}", [128, 512]) for c in range(2)]
                       for p in range(3)]
                pav = [psm(S2, f"pav{i}", [128, 512]) for i in range(2)]

                with nc.Block() as blk:
                    @blk.gpsimd
                    def _(g):
                        g.memset(slop_s[:], 0.0)
                        g.memset(augr_s[:], 0.0)
                        gdma(g, slop_s[0:4, :], D["slopes4"])
                        gdma(g, augr_s[0:4, :], D["augrows"])
                        g.wait_ge(SG.sem, SG.n)

                    @blk.sync
                    def _(s):
                        sdma(s, bias_s[:], D["biascol"])
                        sdma(s, mask_s[:], D["masktiles"])
                        s.wait_ge(SS.sem, SS.n)

                groups = []
                for g_ in range(KVH):
                    for l in range(NL):
                        for hp in range(2):
                            nk = NKP[l]
                            ngr = (nk + 3) // 4
                            for gr in range(ngr):
                                groups.append(dict(
                                    g=g_, l=l, gr=gr, j0=gr * 4,
                                    nj=min(4, nk - gr * 4),
                                    heads=(g_ + hp * 8, g_ + hp * 8 + 4),
                                    first=(gr == 0), last=(gr == ngr - 1)))
                M = len(groups)
                seg_of, segend = {}, {}
                seg = -1
                for m, gr in enumerate(groups):
                    if gr["first"]:
                        seg += 1
                    seg_of[m] = seg
                    if gr["last"]:
                        segend[seg] = m
                NSEG = seg + 1
                SP0, SA0, SV0 = SP.n, SA.n, SV.n
                LAG = 2
                scores_done, sumsav_done = {}, {}
                pe_n = SP0
                for m in range(M):
                    pe_n += 1; scores_done[m] = pe_n
                    if m >= LAG:
                        pe_n += 1; sumsav_done[m - LAG] = pe_n
                for m in range(M - LAG, M):
                    pe_n += 1; sumsav_done[m] = pe_n
                exp_done = {m: SA0 + m + 1 for m in range(M)}
                mask_done, evac_done = {}, {}
                dve_n = SV0
                for s_ in range(NSEG):
                    dve_n += 1; mask_done[segend[s_]] = dve_n
                    dve_n += 2; evac_done[s_] = dve_n

                with nc.Block() as blk:
                    @blk.tensor
                    def _(t):
                        def sums_av(m):
                            gr = groups[m]
                            sg = seg_of[m]
                            t.wait_ge(SA.sem, exp_done[m])
                            if gr["first"] and sg > 1:
                                t.wait_ge(SV.sem, evac_done[sg - 2])
                            p = m % 3
                            pb = pav[sg % 2]
                            for ci in range(2):
                                for jl in range(gr["nj"]):
                                    j = gr["j0"] + jl
                                    ins = t.matmul(
                                        pb[:, ci * 128:(ci + 1) * 128],
                                        v_tok[j][:, gr["g"] * 128:(gr["g"] + 1) * 128],
                                        E[p][ci][:, jl * 128:(jl + 1) * 128],
                                        start=(gr["first"] and ci == 0 and jl == 0),
                                        stop=False,
                                        skip_group_check=True)
                            for ci in range(2):
                                for jl in range(gr["nj"]):
                                    ins = t.matmul(
                                        pb[:, 256 + ci * 128:256 + (ci + 1) * 128],
                                        onesf[:],
                                        E[p][ci][:, jl * 128:(jl + 1) * 128],
                                        start=False,
                                        stop=(gr["last"] and ci == 1 and jl == gr["nj"] - 1),
                                        skip_group_check=True)
                            SP.inc(ins)
                            SP.expect(sumsav_done[m])

                        def augs(m):
                            gr = groups[m]
                            p = m % 3
                            l = gr["l"]
                            for ci, h in enumerate(gr["heads"]):
                                j = gr["j0"]
                                t.matmul(
                                    psc[p][ci][:, 0:gr["nj"] * 128],
                                    slop_s[:, h * 128:(h + 1) * 128],
                                    augr_s[:, (l * 16 + j) * 128:
                                           (l * 16 + j + gr["nj"]) * 128],
                                    start=True, stop=False,
                                    skip_group_check=True)

                        for m, gr in enumerate(groups):
                            p = m % 3
                            l, g_ = gr["l"], gr["g"]
                            if m >= 3:
                                t.wait_ge(SA.sem, exp_done[m - 3])
                            augs(m)
                            for ci, h in enumerate(gr["heads"]):
                                for jl in range(gr["nj"]):
                                    j = gr["j0"] + jl
                                    ins = t.matmul(
                                        psc[p][ci][:, jl * 128:(jl + 1) * 128],
                                        k_fm[g_][:, j * 128:(j + 1) * 128],
                                        q_fm[h][:, l * 128:(l + 1) * 128],
                                        start=False, stop=(jl == gr["nj"] - 1),
                                        skip_group_check=True)
                            SP.inc(ins)
                            SP.expect(scores_done[m])
                            if m >= LAG:
                                sums_av(m - LAG)
                        for m in range(M - LAG, M):
                            sums_av(m)

                    @blk.vector
                    def _(v):
                        for s_ in range(NSEG):
                            m = segend[s_]
                            gr = groups[m]
                            p = m % 3
                            l = gr["l"]
                            o = (gr["nj"] - 2) * 128
                            v.wait_ge(SP.sem, scores_done[m])
                            for ci in range(2):
                                ins = v.tensor_tensor(
                                    psc[p][ci][:, o:o + 256],
                                    psc[p][ci][:, o:o + 256],
                                    mask_s[:, l * 256:(l + 1) * 256], ALU.add)
                            SV.inc(ins)
                            SV.expect(mask_done[m])
                            v.wait_ge(SP.sem, sumsav_done[m])
                            pb = pav[s_ % 2]
                            for ci in range(2):
                                ins = v.reciprocal(inv[ci][:],
                                                   pb[:, 256 + ci * 128:256 + (ci + 1) * 128])
                            vv = SV.inc(ins)
                            v.wait_ge(SV.sem, vv)
                            for ci, h in enumerate(gr["heads"]):
                                ins = v.tensor_tensor(
                                    oa_fm[h][:, l * 128:(l + 1) * 128],
                                    pb[:, ci * 128:(ci + 1) * 128],
                                    inv[ci][:], ALU.mult)
                            SV.inc(ins)
                            SV.expect(evac_done[s_])

                    @blk.scalar
                    def _(sc):
                        for m, gr in enumerate(groups):
                            p = m % 3
                            sc.wait_ge(SP.sem, scores_done[m])
                            if gr["last"]:
                                sc.wait_ge(SV.sem, mask_done[m])
                            for ci, h in enumerate(gr["heads"]):
                                ins = sc.activation(
                                    E[p][ci][:, 0:gr["nj"] * 128],
                                    psc[p][ci][:, 0:gr["nj"] * 128],
                                    AF.Exp, bias=bias_s[:, h:h + 1])
                            SA.inc(ins)
                            SA.expect(exp_done[m])

            if stage == 2:
                with nc.Block() as blk:
                    @blk.sync
                    def _(s):
                        for h in range(H):
                            sdma(s, dbg["attn"][h * 128:(h + 1) * 128, :], oa_fm[h][:])
                        s.wait_ge(SS.sem, SS.n)
                return nc, D

            SCB.close()
            S34 = ExitStack()
            wgb = [sb(S34, f"wgb{i}", [128, C], BF16, side="right") for i in range(3)]
            wub = [sb(S34, f"wub{i}", [128, C], BF16, side="right") for i in range(3)]
            xq_fm = [sb(S34, f"xqf{c}", [128, TL], BF16, side="right")
                     for c in range(16)]
            NPRE = 0 if os.environ.get("DEVK_NOPRE") else 3
            with ExitStack() as S3:
                x1p = [sb(S3, f"x1p{l}", [128, C], F32) for l in range(NL)]
                wob = [sb(S3, f"wob{i}", [128, 16 * 512], BF16, side="right")
                       for i in range(2)]
                x1n = sb(S3, "x1n", [128, C], F32)
                sqs = x1n
                xqb = [sb(S3, f"xqb{i}", [128, C], BF16) for i in range(2)]
                cn = tuple(f"pad{i}" for i in range(16)) + (
                      "sum", "sq", "sg1", "sg2", "sg3", "mean", "var", "msq",
                      "rstd", "nb", "amax",
                      "iam", "s127", "a127", "lnv", "rc", "lnc",
                      "hlv", "ceps")
                col = {nm: sb(S3, "c_" + nm, [128, 1], F32, side="right")
                       for nm in cn}
                py = [psm(S3, f"py{i}", [128, 512]) for i in range(8)]

                SP0, SA0, SV0, SS0, SG0 = SP.n, SA.n, SV.n, SS.n, SG.n
                ymm = lambda m: SP0 + m + 1
                wo_ld = lambda k: SG0 + 16 * (k + 1)
                xres_ld = lambda l: SS0 + 16 * (l + 1)
                x1_st = lambda l: SS0 + 128 + 32 * l + 16
                xq_st = lambda l: SS0 + 128 + 32 * l + 32
                _yevM, _VCH = {}, {}
                _n = SV0
                for _m in range(32):
                    _n += 1
                    _yevM[_m] = _n
                    _k, _lh = divmod(_m, 4)
                    _hf, _fk = divmod(_k, 4)
                    if _fk == 3:
                        _l = _hf * 4 + _lh
                        for _i in range(1, 7):
                            _VCH[(_l, _i)] = _n + _i
                        _n += 6
                yevM = lambda m: _yevM[m]
                VCH = lambda l, i: _VCH[(l, i)]
                ACH = lambda l, i: SA0 + 6 * l + i
                SGW = SG0 + 128
                wldF = lambda hc, mat: SGW + 16 * (hc * 2 + mat + 1)
                XQ7 = SS0 + 128 + 32 * 7 + 32
                XSP7 = SS0 + 128 + 32 * 7 + 16
                xqfF = lambda c, _b=XQ7: _b + 16 * (c + 1)

                with nc.Block() as blk:
                    @blk.gpsimd
                    def _(g):
                        for k in range(8):
                            fk = k % 4
                            if k >= 2:
                                g.wait_ge(SP.sem, ymm((k - 2) * 4 + 3))
                            gdma(g, wob[k % 2][:],
                                 D["wo"][fk * 128:(fk + 1) * 128, :])
                            assert SG.n == wo_ld(k)
                        for hc in range(NPRE):
                            for mat, dr in ((0, "wgt"), (1, "wut")):
                                gdma(g, (wgb if mat == 0 else wub)[hc % 3][:],
                                     D[dr][hc * 128:(hc + 1) * 128, :])
                                assert SG.n == wldF(hc, mat)

                    @blk.sync
                    def _(s):
                        for l in range(NL):
                            sdma(s, x1p[l][:], D["xres"][l * 128:(l + 1) * 128, :])
                            assert SS.n == xres_ld(l)
                        for l in range(NL):
                            s.wait_ge(SV.sem, VCH(l, 3))
                            sdma(s, D["x1d"][l * 128:(l + 1) * 128, :], x1p[l][:])
                            s.wait_ge(SV.sem, VCH(l, 6))
                            sdma(s, D["xqd"][l * 128:(l + 1) * 128, :], xqb[l % 2][:])
                            assert SS.n == xq_st(l)


                    @blk.tensor
                    def _(t):
                        for k in range(8):
                            hf, fk = divmod(k, 4)
                            t.wait_ge(SG.sem, wo_ld(k))
                            for lh in range(4):
                                m = k * 4 + lh
                                l = hf * 4 + lh
                                if m >= 8:
                                    t.wait_ge(SV.sem, yevM(m - 8))
                                t.matmul(py[m % 8][:], ones2p[:],
                                         bo2_s[:, fk * 512:(fk + 1) * 512],
                                         start=True, stop=False,
                                         skip_group_check=True)
                                for h in range(H):
                                    ins = t.matmul(
                                        py[m % 8][:],
                                        oa_fm[h][:, l * 128:(l + 1) * 128],
                                        wob[k % 2][:, h * 512:(h + 1) * 512],
                                        start=False, stop=(h == H - 1),
                                        skip_group_check=True)
                                SP.inc(ins)
                                SP.expect(ymm(m))

                    @blk.vector
                    def _(v):
                        v.memset(col["ceps"][:], -(EPS_LN - EPS_RMS))
                        def evac(m):
                            k, lh = divmod(m, 4)
                            hf, fk = divmod(k, 4)
                            l = hf * 4 + lh
                            v.wait_ge(SP.sem, ymm(m))
                            if fk == 0:
                                v.wait_ge(SS.sem, xres_ld(l))
                            ins = v.tensor_tensor(
                                x1p[l][:, fk * 512:(fk + 1) * 512],
                                x1p[l][:, fk * 512:(fk + 1) * 512],
                                py[m % 8][:], ALU.add)
                            SV.inc(ins)
                            SV.expect(yevM(m))
                        for m in range(32):
                            evac(m)
                            k, lh = divmod(m, 4)
                            hf, fk = divmod(k, 4)
                            if fk != 3:
                                continue
                            l = hf * 4 + lh
                            v.wait_ge(SA.sem, ACH(l, 1))
                            ins = v.reduce_sum(col["sum"][:], x1p[l][:], AX.X)
                            vv = SV.inc(ins)
                            v.wait_ge(SV.sem, vv)
                            v.tensor_scalar_mul(col["mean"][:], col["sum"][:], 1.0 / C)
                            v.tensor_scalar_mul(col["var"][:], col["sq"][:], 1.0 / C)
                            v.tensor_tensor(col["msq"][:], col["mean"][:],
                                            col["mean"][:], ALU.mult)
                            v.tensor_tensor(col["var"][:], col["var"][:],
                                            col["msq"][:], ALU.subtract)
                            ins = v.tensor_scalar_add(col["var"][:], col["var"][:], EPS_LN)
                            SV.inc(ins)
                            v.wait_ge(SA.sem, ACH(l, 3))
                            v.scalar_tensor_tensor(col["nb"][:], col["mean"][:], -1.0,
                                                   col["rstd"][:], ALU.mult, ALU.mult)
                            ins = v.tensor_scalar(x1p[l][:], x1p[l][:], col["rstd"][:],
                                                  col["nb"][:], ALU.mult, ALU.add)
                            SV.inc(ins)
                            v.wait_ge(SA.sem, ACH(l, 6))
                            v.tensor_scalar_mul(x1n[:], x1p[l][:], col["rc"][:])
                            ins = v.tensor_reduce(col["amax"][:], x1n[:], AX.X,
                                                  ALU.max, apply_absolute_value=True)
                            vv = SV.inc(ins)
                            v.wait_ge(SV.sem, vv)
                            ins = v.reciprocal(col["iam"][:], col["amax"][:])
                            vv = SV.inc(ins)
                            v.wait_ge(SV.sem, vv)
                            v.tensor_scalar_mul(col["s127"][:], col["iam"][:], 127.0)
                            v.tensor_scalar_mul(col["a127"][:], col["amax"][:], 1.0 / 127.0)
                            if l >= 2:
                                v.wait_ge(SS.sem, xq_st(l - 2))
                            v.tensor_scalar(x1n[:], x1n[:], col["s127"][:], MAGIC,
                                            ALU.mult, ALU.add)
                            v.tensor_scalar_sub(x1n[:], x1n[:], MAGIC)
                            ins = v.tensor_scalar_mul(xqb[l % 2][:], x1n[:], col["a127"][:])
                            SV.inc(ins)
                            SV.expect(VCH(l, 6))

                    @blk.scalar
                    def _(sc):
                        for l in range(NL):
                            sc.wait_ge(SV.sem, yevM(((l // 4) * 4 + 3) * 4 + l % 4))
                            ins = sc.activation(sqs[:], x1p[l][:], AF.Square,
                                                accum_out=col["sq"][:])
                            SA.inc(ins)
                            sc.wait_ge(SV.sem, VCH(l, 2))
                            ins = sc.activation(col["lnv"][:], col["var"][:], AF.Ln)
                            aa = SA.inc(ins)
                            sc.wait_ge(SA.sem, aa)
                            ins = sc.activation(col["rstd"][:], col["lnv"][:],
                                                AF.Exp, scale=-0.5)
                            SA.inc(ins)
                            ins = sc.activation(col["lnc"][:], col["var"][:],
                                                AF.Ln, bias=col["ceps"][:])
                            aa = SA.inc(ins)
                            ins = sc.activation(col["hlv"][:], col["lnv"][:],
                                                AF.Copy, scale=0.5)
                            aa = SA.inc(ins)
                            sc.wait_ge(SA.sem, aa)
                            ins = sc.activation(col["rc"][:], col["lnc"][:],
                                                AF.Exp, scale=-0.5,
                                                bias=col["hlv"][:])
                            SA.inc(ins)
                            SA.expect(ACH(l, 6))

            if stage == 3:
                dbgc = nc.dram_tensor("dbg_col", [128, len(cn)], F32,
                                      kind="ExternalOutput").ap()
                D["dbg_col"] = dbgc
                with nc.Block() as blk:
                    @blk.sync
                    def _(s):
                        s.wait_ge(SS.sem, SS.n)
                        with nc.allow_non_contiguous_dma(reason="dbg"):
                            for ci, nm in enumerate(cn):
                                sdma(s, dbgc[:, ci:ci + 1], col[nm][:])
                        s.wait_ge(SS.sem, SS.n)
                return nc, D

        with ExitStack() as S4:
            p_fm = [sb(S4, f"pfm{hc}", [128, TL], BF16) for hc in range(64)]
            S4a = S4.enter_context(ExitStack())
            gsg = [sb(S4a, f"gsg{i}", [128, 512], BF16) for i in range(4)]
            pgu = [psm(S4a, f"pgu{i}", [128, 512]) for i in range(8)]
            SP0, SA0, SV0, SS0, SG0 = SP.n, SA.n, SV.n, SS.n, SG.n
            wld = wldF
            sgn_done = lambda hc, mat: SA0 + hc * 4 + mat + 1
            silu_done = lambda hc, th: SA0 + hc * 4 + 2 + th + 1
            gu = lambda hc, mat, th: SP0 + hc * 4 + mat * 2 + th + 1
            pd = lambda hc, th: SV0 + hc * 2 + th + 1

            with nc.Block() as blk:
                @blk.sync
                def _(s):
                    s.wait_ge(SG.sem, wldF(NPRE - 1, 1))
                    s.wait_ge(SS.sem, xqfF(0) - 16)
                    for c in range(16):
                        sdma(s, xq_fm[c][:],
                             D["xqd"][:, c * 128:(c + 1) * 128],
                             transpose=True)
                        assert SS.n == xqfF(c)

                @blk.gpsimd
                def _(g):
                    g.wait_ge(SS.sem, xqfF(15))
                    for hc in range(NPRE, 64):
                        for mat, dr in ((0, "wgt"), (1, "wut")):
                            g.wait_ge(SP.sem, gu(hc - 3, mat, 1))
                            gdma(g, (wgb if mat == 0 else wub)[hc % 3][:],
                                 D[dr][hc * 128:(hc + 1) * 128, :])
                            assert SG.n == wld(hc, mat)

                @blk.scalar
                def _(sc):
                    for hc in range(64):
                        for mat in range(2):
                            wb = (wgb if mat == 0 else wub)[hc % 3]
                            sc.wait_ge(SG.sem, wld(hc, mat))
                            if hc >= 3:
                                sc.wait_ge(SP.sem, gu(hc - 3, mat, 1))
                            ins = sc.activation(wb[:], wb[:], AF.Sign)
                            SA.inc(ins)
                            SA.expect(sgn_done(hc, mat))
                        for th in range(2):
                            sc.wait_ge(SP.sem, gu(hc, 0, th))
                            if hc >= 2:
                                sc.wait_ge(SV.sem, pd(hc - 2, th))
                            bank = 0 * 4 + th * 2 + hc % 2
                            ins = sc.activation(gsg[th * 2 + hc % 2][:],
                                                pgu[bank][:], AF.Silu)
                            SA.inc(ins)
                            SA.expect(silu_done(hc, th))

                @blk.tensor
                def _(t):
                    t.wait_ge(SS.sem, xqfF(15))
                    for hc in range(64):
                        for mat in range(2):
                            wb = (wgb if mat == 0 else wub)[hc % 3]
                            t.wait_ge(SA.sem, sgn_done(hc, mat))
                            for th in range(2):
                                bank = mat * 4 + th * 2 + hc % 2
                                if hc >= 2:
                                    t.wait_ge(SV.sem, pd(hc - 2, th))
                                for c in range(16):
                                    ins = t.matmul(
                                        pgu[bank][:],
                                        wb[:, c * 128:(c + 1) * 128],
                                        xq_fm[c][:, th * 512:(th + 1) * 512],
                                        start=(c == 0), stop=(c == 15))
                                SP.inc(ins)
                                SP.expect(gu(hc, mat, th))

                @blk.vector
                def _(v):
                    for hc in range(64):
                        for th in range(2):
                            v.wait_ge(SA.sem, silu_done(hc, th))
                            v.wait_ge(SP.sem, gu(hc, 1, th))
                            ubank = 1 * 4 + th * 2 + hc % 2
                            ins = v.tensor_tensor(
                                p_fm[hc][:, th * 512:(th + 1) * 512],
                                gsg[th * 2 + hc % 2][:], pgu[ubank][:], ALU.mult)
                            SV.inc(ins)
                            SV.expect(pd(hc, th))

            if stage == 34:
                with nc.Block() as blk:
                    @blk.sync
                    def _(s):
                        for hc in range(64):
                            sdma(s, dbg["p"][hc * 128:(hc + 1) * 128, :], p_fm[hc][:])
                        s.wait_ge(SS.sem, SS.n)
                return nc, D

            S4a.close()
            S34.close()
            with ExitStack() as S5:
                wdb = [sb(S5, f"wdb{i}", [128, 64 * 128], BF16) for i in range(2)]
                x1bf = [sb(S5, f"x1bf{i}", [128, C], BF16) for i in range(NL)]
                col6 = {nm: sb(S5, "d_" + nm, [128, 1], F32) for nm in
                        ("sum", "sq", "mean", "var", "msq", "rstd", "nb", "lnv")}
                pdn = [psm(S5, f"pdn{i}", [128, 512]) for i in range(4)]
                SP0, SA0, SV0, SS0, SG0 = SP.n, SA.n, SV.n, SS.n, SG.n
                wdld = lambda sx: (SG0 + 16 * (sx + 1) if sx < 2
                                   else SG0 + 160 + 16 * (sx - 1))
                wsgn = lambda sx: SA0 + sx + 1
                dn = lambda m: SP0 + m + 1
                x1bf_ld = lambda l: SG0 + 32 + 16 * (l + 1)
                y_st = lambda l: SG0 + 384 + 16 * (l + 1)
                _yev5, _V6 = {}, {}
                _n = SV0
                for _m in range(128):
                    _n += 1
                    _yev5[_m] = _n
                    if _m >= 120:
                        _l = _m - 120
                        for _i in range(1, 4):
                            _V6[(_l, _i)] = _n + _i
                        _n += 3
                yev5 = lambda m: _yev5[m]
                V6_ = lambda l, i: _V6[(l, i)]
                A6_ = lambda l, i: SA0 + 16 + 3 * l + i

                with nc.Block() as blk:
                    @blk.gpsimd
                    def _(g):
                        for sx in range(2):
                            gdma(g, wdb[sx % 2][:],
                                 D["wdt"][sx * 128:(sx + 1) * 128, :])
                            assert SG.n == wdld(sx)
                        g.wait_ge(SS.sem, XSP7)
                        for l in range(NL):
                            gdma(g, x1bf[l][:], D["x1d"][l * 128:(l + 1) * 128, :])
                            assert SG.n == x1bf_ld(l)
                        for sx in range(2, 16):
                            g.wait_ge(SP.sem, dn((sx - 2) * 8 + 7))
                            gdma(g, wdb[sx % 2][:],
                                 D["wdt"][sx * 128:(sx + 1) * 128, :])
                            assert SG.n == wdld(sx)
                        for l in range(NL):
                            g.wait_ge(SV.sem, V6_(l, 3))
                            gdma(g, D["y"][l * 128:(l + 1) * 128, :], x1bf[l][:])
                            assert SG.n == y_st(l)
                        g.wait_ge(SG.sem, SG.n)

                    @blk.scalar
                    def _(sc):
                        for sx in range(16):
                            sc.wait_ge(SG.sem, wdld(sx))
                            if sx >= 2:
                                sc.wait_ge(SP.sem, dn((sx - 2) * 8 + 7))
                            ins = sc.activation(wdb[sx % 2][:], wdb[sx % 2][:], AF.Sign)
                            SA.inc(ins)
                            SA.expect(wsgn(sx))
                        for l in range(NL):
                            sc.wait_ge(SV.sem, V6_(l, 1))
                            ins = sc.activation(wdb[0][:, 0:C], x1bf[l][:], AF.Square,
                                                accum_out=col6["sq"][:])
                            SA.inc(ins)
                            sc.wait_ge(SV.sem, V6_(l, 2))
                            ins = sc.activation(col6["lnv"][:], col6["var"][:], AF.Ln)
                            aa = SA.inc(ins)
                            sc.wait_ge(SA.sem, aa)
                            ins = sc.activation(col6["rstd"][:], col6["lnv"][:],
                                                AF.Exp, scale=-0.5)
                            SA.inc(ins)
                            SA.expect(A6_(l, 3))

                    @blk.tensor
                    def _(t):
                        for sx in range(16):
                            t.wait_ge(SA.sem, wsgn(sx))
                            for l in range(NL):
                                m = sx * 8 + l
                                if m >= 4:
                                    t.wait_ge(SV.sem, yev5(m - 4))
                                for hc in range(64):
                                    ins = t.matmul(
                                        pdn[m % 4][:, 0:128],
                                        p_fm[hc][:, l * 128:(l + 1) * 128],
                                        wdb[sx % 2][:, hc * 128:(hc + 1) * 128],
                                        start=(hc == 0), stop=(hc == 63))
                                SP.inc(ins)
                                SP.expect(dn(m))

                    @blk.vector
                    def _(v):
                        for m in range(128):
                            sx_, l_ = divmod(m, 8)
                            v.wait_ge(SP.sem, dn(m))
                            if m == 0:
                                v.wait_ge(SG.sem, x1bf_ld(7))
                            ins = v.scalar_tensor_tensor(
                                x1bf[l_][:, sx_ * 128:(sx_ + 1) * 128],
                                pdn[m % 4][:, 0:128], 1.0,
                                x1bf[l_][:, sx_ * 128:(sx_ + 1) * 128],
                                ALU.mult, ALU.add)
                            SV.inc(ins)
                            SV.expect(yev5(m))
                            if m < 120:
                                continue
                            l = m - 120
                            ins = v.reduce_sum(col6["sum"][:], x1bf[l][:], AX.X)
                            vv = SV.inc(ins)
                            v.wait_ge(SV.sem, vv)
                            v.wait_ge(SA.sem, A6_(l, 1))
                            v.tensor_scalar_mul(col6["mean"][:], col6["sum"][:], 1.0 / C)
                            v.tensor_scalar_mul(col6["var"][:], col6["sq"][:], 1.0 / C)
                            v.tensor_tensor(col6["msq"][:], col6["mean"][:],
                                            col6["mean"][:], ALU.mult)
                            v.tensor_tensor(col6["var"][:], col6["var"][:],
                                            col6["msq"][:], ALU.subtract)
                            ins = v.tensor_scalar_add(col6["var"][:], col6["var"][:],
                                                      EPS_LN)
                            SV.inc(ins)
                            v.wait_ge(SA.sem, A6_(l, 3))
                            v.scalar_tensor_tensor(col6["nb"][:], col6["mean"][:], -1.0,
                                                   col6["rstd"][:], ALU.mult, ALU.mult)
                            ins = v.tensor_scalar(x1bf[l][:], x1bf[l][:],
                                                  col6["rstd"][:], col6["nb"][:],
                                                  ALU.mult, ALU.add)
                            SV.inc(ins)
                            SV.expect(V6_(l, 3))

    return nc, D


def kernel(**inputs):
    nc, _ = build(stage=5)
    in_maps = [stage_core(inputs, c) for c in range(8)]
    res = run_bass_kernel_spmd(nc, in_maps, core_ids=list(range(8)))
    out = np.empty((B, T, C), dtype=np.float32)
    for c in range(8):
        b, par = c // 2, c % 2
        y = np.asarray(res.results[c]["y"])
        for li, g in enumerate(TILES[par]):
            out[b, g * 128:(g + 1) * 128, :] = y[li * 128:(li + 1) * 128, :]
    return out
